# version 09
# baseline (speedup 1.0000x reference)
import numpy as np
import concourse.bass as bass
import concourse.mybir as mybir

F32 = mybir.dt.float32
BF16 = mybir.dt.bfloat16
AF = mybir.ActivationFunctionType
ALU = mybir.AluOpType
AX = mybir.AxisListType

D = 1024
HEADS = 16
DH = 64
NE = 32
EI = 512
SEQT = 32
OWNT = 16
NQ = 4
QT = 4
TQ = 512
HC = 8
LN_EPS = 1e-5
SCALE = 8.0

ENGINES = ("sync", "tensor", "vector", "scalar", "gpsimd")


class Tok:
    __slots__ = ("eng", "dma", "need", "val", "key")
    def __init__(self, eng, dma):
        self.eng = eng
        self.dma = dma
        self.need = dma
        self.val = None
        self.key = None


class Res:
    __slots__ = ("w", "readers", "init", "semkey")
    def __init__(self, init=()):
        self.w = None
        self.readers = []
        self.init = list(init)
        self.semkey = None
    def rdeps(self):
        return [self.w] if self.w is not None else list(self.init)
    def wdeps(self):
        base = [self.w] if self.w is not None else list(self.init)
        return base + self.readers
    def did_read(self, tok):
        self.readers.append(tok)
    def did_write(self, tok):
        self.w = tok
        self.readers = []
    def drain(self):
        out = list(self.init) + self.readers
        if self.w is not None:
            out.append(self.w)
        return out


class P:
    def __init__(self, nc):
        self.nc = nc
        self.order = []
        self.dsems = set()

    def op(self, eng, fn, reads=(), writes=(), also=(), dma=False, dsem=None):
        deps = []
        for r in reads:
            deps += r.rdeps()
        for w in writes:
            deps += w.wdeps()
        deps += [t for t in also if t is not None]
        tok = Tok(eng, dma)
        if dma:
            if dsem is None:
                assert writes, "dma op needs a target Res or explicit dsem"
                key = getattr(writes[0], "semkey", None)
                if key is None:
                    key = f"dR{len(self.dsems)}"
                    writes[0].semkey = key
                    self.dsems.add(key)
                dsem = key
            tok.key = dsem
        self.order.append((eng, fn, deps, tok))
        for r in reads:
            r.did_read(tok)
        for w in writes:
            w.did_write(tok)
        return tok

    def finalize_and_emit(self):
        nc = self.nc
        PIPELINED = ("vector", "scalar")
        for eng, fn, deps, tok in self.order:
            for d in deps:
                if d.eng != eng or d.dma or eng in PIPELINED:
                    d.need = True
        semnames = set()
        for eng, fn, deps, tok in self.order:
            if tok.need:
                if not tok.dma:
                    tok.key = "c_" + eng
                semnames.add(tok.key)
        sems = {k: nc.alloc_semaphore(name=k) for k in sorted(semnames)}
        cnt = {k: 0 for k in semnames}
        for eng, fn, deps, tok in self.order:
            if tok.need:
                cnt[tok.key] += 16 if tok.dma else 1
                tok.val = cnt[tok.key]
        per_eng = {e: [] for e in ENGINES}
        maxwait = {e: {} for e in ENGINES}
        nwaits = 0
        for i, (eng, fn, deps, tok) in enumerate(self.order):
            waits = []
            for d in deps:
                if d.eng == eng and not d.dma and eng not in PIPELINED:
                    continue
                mw = maxwait[eng]
                if mw.get(d.key, 0) >= d.val:
                    continue
                mw[d.key] = d.val
                waits.append((d.key, d.val))
            nwaits += len(waits)
            per_eng[eng].append((waits, fn, tok))
        with nc.Block() as blk:
            def mk(eng_name):
                def body(eng):
                    for waits, fn, tok in per_eng[eng_name]:
                        for k, v in waits:
                            eng.wait_ge(sems[k], v)
                        if fn is None:
                            continue
                        inst = fn(eng)
                        if tok.need:
                            inst.then_inc(sems[tok.key], 16 if tok.dma else 1)
                return body
            blk.sync(mk("sync"))
            blk.tensor(mk("tensor"))
            blk.vector(mk("vector"))
            blk.scalar(mk("scalar"))
            blk.gpsimd(mk("gpsimd"))
        n_inst = sum(len(v) for v in per_eng.values())
        return n_inst, nwaits


class Arena:
    def __init__(self, nc):
        self.nc = nc
        self.cur = nc.sbuf_base
        self.top = nc.sbuf_top
        self.n = 0

    def alloc_at(self, off, name, shape, dtype):
        self.n += 1
        t = self.nc.alloc_sbuf_tensor_at(f"{name}_{self.n}", list(shape), dtype, offset=off)
        return t.ap()

    def alloc(self, name, shape, dtype):
        per_part = int(np.prod(shape[1:])) * mybir.dt.size(dtype)
        off = (self.cur + 63) & ~63
        if off + per_part > self.top:
            raise AssertionError(f"SBUF overflow: {name} needs {per_part}B at {off}, top={self.top}")
        if per_part >= 1024:
            print(f"  alloc {name}: {per_part}B at {off}")
        self.cur = off + per_part
        return self.alloc_at(off, name, shape, dtype), None

    def reserve(self, nbytes):
        off = (self.cur + 63) & ~63
        assert off + nbytes <= self.top
        self.cur = off + nbytes
        return off


class Region:
    def __init__(self, arena, nbytes):
        self.arena = arena
        self.base = arena.reserve(nbytes)
        self.size = nbytes
        self.cur = self.base
        self.gen = []
        self.pending = []

    def alloc(self, name, shape, dtype):
        per_part = int(np.prod(shape[1:])) * mybir.dt.size(dtype)
        off = (self.cur + 63) & ~63
        assert off + per_part <= self.base + self.size, (
            f"Region overflow: {name} needs {per_part}B at {off - self.base}, size {self.size}")
        self.cur = off + per_part
        ap = self.arena.alloc_at(off, name, shape, dtype)
        r = Res(init=self.pending)
        r.semkey = f"dG{off}"
        self.gen.append(r)
        return ap, r

    def extra(self, key=None):
        r = Res(init=self.pending)
        if key is not None:
            r.semkey = "dX" + key
        self.gen.append(r)
        return r

    def reset(self):
        drains = []
        for r in self.gen:
            drains += r.drain()
        seen = set()
        out = []
        for t in drains:
            if id(t) not in seen:
                seen.add(id(t))
                out.append(t)
        self.pending = out
        self.gen = []
        self.cur = self.base


def build(nc, mode="f32", debug=()):
    assert mode == "f32"
    wdt = F32

    p = P(nc)
    A = Arena(nc)

    def din(name, shape, dt=F32):
        return nc.dram_tensor(name, list(shape), dt, kind="ExternalInput").ap()

    x_own = din("x_own", [OWNT, 128, D])
    x_seq = din("x_seq", [SEQT, 128, D])
    wq_d = din("wq", [D, D])
    wkv_d = din("wkv", [D, 2 * DH])
    wout_d = din("wout", [D, D])
    gammab_d = din("gammab", [128, D])
    qs_col_d = din("qs_col", [128, 1])
    ks_col_d = din("ks_col", [64, 1])
    qs_row_d = din("qs_row", [1, DH])
    ks_row_d = din("ks_row", [1, DH])
    ks2_d = din("ks2", [2, DH])
    nullk_d = din("nullk", [2, DH])
    nullv_d = din("nullv", [2, DH])
    gate_w_d = din("gate_w", [NE, D])
    biasb_d = din("biasb", [128, NE])
    eg_d = din("eg", [NE, EI, D])
    eu_d = din("eu", [NE, EI, D])
    ed_d = din("ed", [NE, D, EI])
    sg_d = din("sg", [D, D])
    su_d = din("su", [D, D])
    sd_d = din("sd", [D, D])
    ident_d = din("ident", [128, 128])
    pool2_d = din("pool2", [128, 2])
    hm2_d = din("hm2", [34, 128])
    ones64r_d = din("ones64r", [1, 64])
    ones128r_d = din("ones128r", [1, 128])
    maskA_d = din("maskA", [128, 128])
    maskA0_d = din("maskA0", [128, 128])
    maskB_d = din("maskB", [128, 128])
    dup64_d = din("dup64", [64, 128])

    out_d = nc.dram_tensor("out", [OWNT, 128, D], F32, kind="ExternalOutput").ap()

    def dpair(name, shape):
        return nc.dram_tensor(name, [2] + list(shape), BF16, kind="Internal").ap()
    egT = nc.dram_tensor("egT", [NE, 2, D, EI], BF16, kind="Internal").ap()
    euT = nc.dram_tensor("euT", [NE, 2, D, EI], BF16, kind="Internal").ap()
    edT = nc.dram_tensor("edT", [NE, 2, EI, D], BF16, kind="Internal").ap()
    sgT = dpair("sgT", [D, D])
    suT = dpair("suT", [D, D])
    sdT = dpair("sdT", [D, D])
    wqS = dpair("wqS", [D, D])
    woS = dpair("woS", [D, D])

    pb = [nc.alloc_psum_tensor(f"pb{i}", [128, 512], F32).ap() for i in range(8)]
    pres = [Res() for _ in range(8)]

    def S(name, shape, dtype=F32):
        ap, _ = A.alloc(name, shape, dtype)
        return ap, Res()

    def SP(name, shape):
        h, _ = A.alloc(name + "h", shape, BF16)
        l, _ = A.alloc(name + "l", shape, BF16)
        return h, l, Res()

    acc, r_acc = S("acc", [128, QT * D])
    xnf_h, xnf_l, r_xnf = SP("xnf", [128, HC * TQ])
    kbuf_h, kbuf_l, r_kbuf = SP("kbuf", [128, SEQT * 128])
    vbuf_h, vbuf_l, r_vbuf = SP("vbuf", [128, SEQT * 65])
    xa, r_xa = S("xa", [128, D])
    xb, r_xb = S("xb", [128, D])
    xc_, r_xc = S("xc", [128, D])
    xd, r_xd = S("xd", [128, D])
    lnS, r_lnS = S("lnS", [128, 32])
    r_lnS2 = Res()
    sA, r_sA = S("sA", [34, 512])
    sB, r_sB = S("sB", [34, 512])
    sC, r_sC = S("sC", [34, 512])
    xn2f_h, xn2f_l, r_xn2f = SP("xn2f", [128, HC * TQ])
    q2, r_q2 = S("q2", [128, 512])
    rnb, r_rnb = S("rnb", [128, 512])
    kq, r_kq = S("kq", [128, 512])
    psbF = [S("psbF0", [128, 512]), S("psbF1", [128, 512])]
    psbP = [SP("psbP0", [128, 512]), SP("psbP1", [128, 512])]
    pn_h, pn_l, r_pn = SP("pn", [34, 512])
    zb, r_zb = S("zb", [64, 512])
    g32a, r_g32a = S("g32a", [128, NE])
    g32b, r_g32b = S("g32b", [128, NE])
    g32c, r_g32c = S("g32c", [128, NE])
    g4a, r_g4a = S("g4a", [128, 4])
    g4b, r_g4b = S("g4b", [128, 4])
    g1a, r_g1a = S("g1a", [128, 1])
    g8a, r_g8a = S("g8a", [128, 8])
    g8pad, r_g8pad = S("g8pad", [128, 8])
    gsil, r_gsil = S("gsil", [128, 512])
    hsjF, r_hsjF = S("hsjF", [128, 512])
    heF, r_heF = hsjF, r_hsjF
    hsjb = [SP("hsj0", [128, 512]), SP("hsj1", [128, 512])]
    heb = [SP("he0", [128, 2 * 512]), SP("he1", [128, 2 * 512])]
    wf, r_wf = S("wf", [128, QT * NE])
    gwT_h, gwT_l, r_gwT = SP("gwT", [128, HC * NE])
    wkv_h, wkv_l, r_wkv = SP("wkv_sb", [128, HC * 2 * DH])
    gamma_sb, r_gamma = S("gamma", [128, D])
    maskA_sb, r_maskA = S("maskA", [128, 128], BF16)
    maskA0_sb, r_maskA0 = S("maskA0", [128, 128], BF16)
    maskB_sb, r_maskB = S("maskB", [128, 128], BF16)
    ident_sb, r_ident = S("ident", [128, 128])
    pool2_sb, r_pool2 = S("pool2", [128, 2])
    hm2_sb, r_hm2 = S("hm2", [34, 128])
    ones64r_sb, r_o64 = S("ones64r", [1, 64])
    ones128r_sb, r_o128 = S("ones128r", [1, 128])
    dup64_sb, r_dup = S("dup64", [64, 128])
    qs_col_sb, r_qsc = S("qs_col", [128, 1])
    ks_col_sb, r_ksc = S("ks_col", [64, 1])
    qs_row_sb, r_qsr = S("qs_row", [1, DH])
    ks_row_sb, r_ksr = S("ks_row", [1, DH])
    ks2_sb, r_ks2 = S("ks2", [2, DH])
    nullk_sb, r_nullk = S("nullk", [2, DH])
    nkhT4_h, nkhT4_l, r_nkhT = SP("nkhT4", [128, 34])
    nva_h, nva_l, r_nva = SP("nva", [34, 65])
    nkh64, r_nkh64 = S("nkh64", [64, 2])
    negB_sb, r_negB = S("negB", [128, 1])
    biasb_sb, r_biasb = S("biasb", [128, NE])

    qhat_h, qhat_l, r_qhat = SP("qhat", [128, HC * TQ])

    REG = Region(A, 48 * 1024)
    print(f"static SBUF: {A.cur - REG.size} B/part + region {REG.size}; top={A.top}")

    def dma(dst, src, reads=(), writes=(), also=(), q="sync", dsem=None):
        return p.op(q, lambda e, d=dst, s=src: e.dma_start(out=d, in_=s),
                    reads=reads, writes=writes, also=also, dma=True, dsem=dsem)

    def vop(fn, reads=(), writes=(), also=()):
        return p.op("vector", fn, reads=reads, writes=writes, also=also)

    def sop(fn, reads=(), writes=(), also=()):
        return p.op("scalar", fn, reads=reads, writes=writes, also=also)

    def top(fn, reads=(), writes=(), also=()):
        return p.op("tensor", fn, reads=reads, writes=writes, also=also)

    def mmgroup(ps_ap, bank_i, pairs, reads):
        n = len(pairs)
        for i, (lh, rh) in enumerate(pairs):
            st = (i == 0)
            sp = (i == n - 1)
            top(lambda e, lh=lh, rh=rh, st=st, sp=sp: e.matmul(ps_ap, lh, rh, start=st, stop=sp),
                reads=list(reads), writes=[pres[bank_i]])

    def mmgroup3(ps_ap, bank_i, pairs4, reads):
        flat = []
        for lh_hi, lh_lo, rh_hi, rh_lo in pairs4:
            flat.append((lh_hi, rh_hi))
            flat.append((lh_hi, rh_lo))
            flat.append((lh_lo, rh_hi))
        n = len(flat)
        for i, (lh, rh) in enumerate(flat):
            st = (i == 0)
            sp = (i == n - 1)
            top(lambda e, lh=lh, rh=rh, st=st, sp=sp: e.matmul(ps_ap, lh, rh, start=st, stop=sp),
                reads=list(reads), writes=[pres[bank_i]])

    def vsplit(src_ap, rsrc, hi_ap, lo_ap, rdst, on_scalar_hi=False):
        if on_scalar_hi:
            sop(lambda e: e.activation(hi_ap, src_ap, AF.Identity), reads=[rsrc], writes=[rdst])
        else:
            vop(lambda e: e.tensor_copy(hi_ap, src_ap), reads=[rsrc], writes=[rdst])
        vop(lambda e: e.scalar_tensor_tensor(lo_ap, src_ap, 1.0, hi_ap, ALU.mult, ALU.subtract),
            reads=[rsrc, rdst], writes=[rdst])

    dma(gamma_sb, gammab_d, writes=[r_gamma])
    dma(maskA_sb, maskA_d, writes=[r_maskA], q="gpsimd")
    dma(maskA0_sb, maskA0_d, writes=[r_maskA0], q="gpsimd")
    dma(maskB_sb, maskB_d, writes=[r_maskB], q="gpsimd")
    dma(ident_sb, ident_d, writes=[r_ident])
    dma(pool2_sb, pool2_d, writes=[r_pool2])
    dma(hm2_sb, hm2_d, writes=[r_hm2])
    dma(ones64r_sb, ones64r_d, writes=[r_o64])
    dma(ones128r_sb, ones128r_d, writes=[r_o128])
    dma(dup64_sb, dup64_d, writes=[r_dup])
    dma(qs_col_sb, qs_col_d, writes=[r_qsc])
    dma(ks_col_sb, ks_col_d, writes=[r_ksc])
    dma(qs_row_sb, qs_row_d, writes=[r_qsr])
    dma(ks_row_sb, ks_row_d, writes=[r_ksr])
    dma(ks2_sb, ks2_d, writes=[r_ks2])
    dma(nullk_sb, nullk_d, writes=[r_nullk])
    nvaF, r_nvaF = S("nvaF", [2, 64])
    dma(nvaF, nullv_d, writes=[r_nvaF])
    dma(biasb_sb, biasb_d, writes=[r_biasb])
    dma(xa[:, 0:HC * 2 * DH].rearrange("p (a f) -> p a f", a=HC),
        wkv_d.rearrange("(a p) f -> p a f", p=128), writes=[r_xa])
    vsplit(xa[:, 0:HC * 2 * DH], r_xa, wkv_h, wkv_l, r_wkv)
    vsplit(nvaF, r_nvaF, nva_h[0:2, 0:64], nva_l[0:2, 0:64], r_nva)
    dma(nva_h[32:34, 0:64], nullv_d, writes=[r_nva], q="gpsimd")
    nvaF2 = kq[32:34, 0:64]
    dma(nvaF2, nullv_d, writes=[r_kq])
    vop(lambda e: e.scalar_tensor_tensor(nva_l[32:34, 0:64], nvaF2, 1.0,
                                         nva_h[32:34, 0:64], ALU.mult, ALU.subtract),
        reads=[r_kq, r_nva], writes=[r_nva])

    vop(lambda e: e.memset(vbuf_h, 1.0), writes=[r_vbuf])
    vop(lambda e: e.memset(vbuf_l, 0.0), writes=[r_vbuf])
    vop(lambda e: e.memset(nva_h[0:2, 64:65], 1.0), writes=[r_nva])
    vop(lambda e: e.memset(nva_l[0:2, 64:65], 0.0), writes=[r_nva])
    vop(lambda e: e.memset(nva_h[32:34, 64:65], 1.0), writes=[r_nva])
    vop(lambda e: e.memset(nva_l[32:34, 64:65], 0.0), writes=[r_nva])
    vop(lambda e: e.memset(g8pad, -1e30), writes=[r_g8pad])

    vop(lambda e: e.tensor_reduce(sA[0:1, 0:1], qs_row_sb, AX.X, ALU.max, apply_absolute_value=True),
        reads=[r_qsr], writes=[r_sA])
    vop(lambda e: e.tensor_reduce(sA[0:1, 1:2], ks_row_sb, AX.X, ALU.max, apply_absolute_value=True),
        reads=[r_ksr], writes=[r_sA])
    vop(lambda e: e.tensor_tensor(sA[0:1, 2:3], sA[0:1, 0:1], sA[0:1, 1:2], ALU.mult),
        reads=[r_sA], writes=[r_sA])
    vop(lambda e: e.tensor_scalar(sA[0:1, 3:4], sA[0:1, 2:3], -8.0, None, ALU.mult),
        reads=[r_sA], writes=[r_sA])
    top(lambda e: e.matmul(pb[0][0:128, 0:1], ones128r_sb, sA[0:1, 3:4], start=True, stop=True),
        reads=[r_sA, r_o128], writes=[pres[0]])
    vop(lambda e: e.tensor_copy(negB_sb, pb[0][0:128, 0:1]), reads=[pres[0]], writes=[r_negB])

    ph0n = []
    ph0o = []
    for i in range(2):
        ap, r = REG.alloc(f"ph0n{i}", [128, 2 * D], F32)
        ph0n.append((ap, r))
        ap, r = REG.alloc(f"ph0o{i}", [128, 2 * D], F32)
        ph0o.append((ap, r))

    jobs = []
    for src2, dstS in [(wq_d, wqS), (wout_d, woS)]:
        for roff in range(0, D, 256):
            jobs.append(("s", src2[roff:roff + 256, :], dstS, roff, 256, D, "d_ph0a"))
    for src2, dstT, Rtot, C in [(sg_d, sgT, D, D), (su_d, suT, D, D), (sd_d, sdT, D, D)]:
        for roff in range(0, Rtot, 256):
            jobs.append(("t", src2[roff:roff + 256, :], dstT, roff, 256, C, "d_ph0a"))
    for e in range(NE):
        for src3, dstT, Rtot, C in [(eg_d, egT, EI, D), (eu_d, euT, EI, D), (ed_d, edT, D, EI)]:
            for roff in range(0, Rtot, 256):
                jobs.append(("t", src3[e][roff:roff + 256, :],
                             (dstT[e][0], dstT[e][1]), roff, 256, C, "d_ph0b"))

    ph0_last = {}
    for ji, (kind, src2d, dst2d, roff, Rr, C, grp) in enumerate(jobs):
        nat, natr = ph0n[ji % 2]
        outb, outr = ph0o[ji % 2]
        nrr = Rr // 128
        ncc = C // 128
        dma(nat[:, 0:nrr * C].rearrange("p (a f) -> p a f", a=nrr),
            src2d.rearrange("(a p) f -> p a f", p=128), writes=[natr])
        obh = outb.bitcast(BF16)
        HB = 2 * C
        if kind == "t":
            for cg in range(0, ncc, 2):
                bank = 6 + (cg // 2) % 2
                for cc in range(cg, min(cg + 2, ncc)):
                    for rc in range(nrr):
                        def tf(e, nat=nat, rc=rc, cc=cc, cg=cg, C=C, bank=bank):
                            col = ((cc - cg) * nrr + rc) * 128
                            return e.transpose(pb[bank][:, col:col + 128],
                                               nat[:, rc * C + cc * 128: rc * C + cc * 128 + 128],
                                               ident_sb)
                        top(tf, reads=[natr, r_ident], writes=[pres[bank]])
                ntile = (min(cg + 2, ncc) - cg) * nrr
                def evh(e, cg=cg, Rr=Rr, bank=bank, ntile=ntile, obh=obh):
                    return e.tensor_copy(obh[:, cg * Rr: cg * Rr + ntile * 128],
                                         pb[bank][:, 0:ntile * 128])
                vop(evh, reads=[pres[bank]], writes=[outr])
                def evl(e, cg=cg, Rr=Rr, bank=bank, ntile=ntile, obh=obh, HB=HB):
                    return e.scalar_tensor_tensor(
                        obh[:, HB + cg * Rr: HB + cg * Rr + ntile * 128],
                        pb[bank][:, 0:ntile * 128], 1.0,
                        obh[:, cg * Rr: cg * Rr + ntile * 128], ALU.mult, ALU.subtract)
                vop(evl, reads=[pres[bank], outr], writes=[outr])
        else:
            def evh2(e, nat=nat, nrr=nrr, C=C, obh=obh):
                return e.tensor_copy(obh[:, 0:nrr * C], nat[:, 0:nrr * C])
            vop(evh2, reads=[natr], writes=[outr])
            def evl2(e, nat=nat, nrr=nrr, C=C, obh=obh, HB=HB):
                return e.scalar_tensor_tensor(obh[:, HB:HB + nrr * C], nat[:, 0:nrr * C],
                                              1.0, obh[:, 0:nrr * C], ALU.mult, ALU.subtract)
            vop(evl2, reads=[natr, outr], writes=[outr])
        if kind == "t":
            dsth, dstl = dst2d[0], dst2d[1]
            ph0_last.__setitem__(grp, dma(dsth[:, roff:roff + Rr].rearrange("(a p) f -> p a f", p=128),
                obh[:, 0:ncc * Rr].rearrange("p (a f) -> p a f", a=ncc),
                reads=[outr], q="gpsimd", dsem=grp))
            ph0_last.__setitem__(grp, dma(dstl[:, roff:roff + Rr].rearrange("(a p) f -> p a f", p=128),
                obh[:, HB:HB + ncc * Rr].rearrange("p (a f) -> p a f", a=ncc),
                reads=[outr], q="gpsimd", dsem=grp))
        else:
            dsth, dstl = dst2d[0], dst2d[1]
            ph0_last.__setitem__(grp, dma(dsth[roff:roff + Rr, :].rearrange("(a p) f -> p a f", p=128),
                obh[:, 0:nrr * C].rearrange("p (a f) -> p a f", a=nrr),
                reads=[outr], q="gpsimd", dsem=grp))
            ph0_last.__setitem__(grp, dma(dstl[roff:roff + Rr, :].rearrange("(a p) f -> p a f", p=128),
                obh[:, HB:HB + nrr * C].rearrange("p (a f) -> p a f", a=nrr),
                reads=[outr], q="gpsimd", dsem=grp))

    def rsqrt2(src_ap, dst_ap, rows, rr, sr=0):
        w = src_ap.shape[1]
        b_ = sB[sr:sr + rows, 0:w]
        c_ = sC[sr:sr + rows, 0:w]
        vop(lambda e: e.tensor_scalar_max(b_, src_ap, 1e-24), reads=[rr], writes=[r_sB])
        sop(lambda e: e.activation(c_, b_, AF.Sqrt), reads=[r_sB], writes=[r_sC])
        vop(lambda e: e.reciprocal(dst_ap, c_), reads=[r_sC], writes=[rr])
        vop(lambda e: e.tensor_tensor(c_, b_, dst_ap, ALU.mult), reads=[r_sB, rr], writes=[r_sC])
        vop(lambda e: e.tensor_tensor(c_, c_, dst_ap, ALU.mult), reads=[rr], writes=[r_sC])
        vop(lambda e: e.tensor_scalar(c_, c_, -0.5, 1.5, ALU.mult, ALU.add), reads=[r_sC], writes=[r_sC])
        vop(lambda e: e.tensor_tensor(dst_ap, dst_ap, c_, ALU.mult), reads=[r_sC, rr], writes=[rr])

    sop(lambda e: e.activation(sA[0:2, 64:128], nullk_sb, AF.Square),
        reads=[r_nullk], writes=[r_sA])
    vop(lambda e: e.tensor_reduce(sA[0:2, 128:129], sA[0:2, 64:128], AX.X, ALU.add),
        reads=[r_sA], writes=[r_sA])
    rsqrt2(sA[0:2, 128:129], sA[0:2, 129:130], 2, r_sA)
    vop(lambda e: e.scalar_tensor_tensor(sA[0:2, 192:256], nullk_sb, sA[0:2, 129:130], ks2_sb,
                                         ALU.mult, ALU.mult),
        reads=[r_sA, r_nullk, r_ks2], writes=[r_sA])
    top(lambda e: e.transpose(pb[0][0:64, 0:2], sA[0:2, 192:256], ident_sb[0:2, 0:2]),
        reads=[r_sA, r_ident], writes=[pres[0]])
    vop(lambda e: e.tensor_copy(nkh64, pb[0][0:64, 0:2]), reads=[pres[0]], writes=[r_nkh64])
    top(lambda e: e.matmul(pb[1][0:128, 0:2], dup64_sb, nkh64, start=True, stop=True),
        reads=[r_nkh64, r_dup], writes=[pres[1]])
    vop(lambda e: e.memset(nkhT4_h, 0.0), writes=[r_nkhT])
    vop(lambda e: e.memset(nkhT4_l, 0.0), writes=[r_nkhT])
    vsplit(pb[1][0:64, 0:2], pres[1], nkhT4_h[0:64, 0:2], nkhT4_l[0:64, 0:2], r_nkhT)
    vsplit(pb[1][64:128, 0:2], pres[1], nkhT4_h[64:128, 32:34], nkhT4_l[64:128, 32:34], r_nkhT)

    gwn, r_gwn = REG.alloc("gwn", [32, D], F32)
    dma(gwn, gate_w_d, writes=[r_gwn])
    for hc in range(HC):
        top(lambda e, hc=hc: e.transpose(pb[1][0:128, hc * NE:(hc + 1) * NE],
                                         gwn[:, hc * 128:(hc + 1) * 128], ident_sb[0:32, 0:32]),
            reads=[r_gwn, r_ident], writes=[pres[1]])
    vsplit(pb[1][:, 0:HC * NE], pres[1], gwT_h, gwT_l, r_gwT)

    def emit_ln(x_ap, rx, xcdst, rxc, sqdst, rsq, xndst, rxn, par=0):
        c = 16 * par
        rl = r_lnS if par == 0 else r_lnS2
        vop(lambda e, c=c: e.tensor_reduce(lnS[:, c:c + 1], x_ap, AX.X, ALU.add),
            reads=[rx], writes=[rl])
        sop(lambda e, c=c: e.activation(sqdst, x_ap, AF.Square, accum_out=lnS[:, c + 1:c + 2]),
            reads=[rx], writes=[rsq, rl])
        vop(lambda e, c=c: e.tensor_scalar(lnS[:, c + 2:c + 3], lnS[:, c:c + 1], 1.0 / D, None, ALU.mult),
            reads=[rl], writes=[rl])
        vop(lambda e, c=c: e.tensor_tensor(lnS[:, c + 3:c + 4], lnS[:, c + 2:c + 3], lnS[:, c + 2:c + 3], ALU.mult),
            reads=[rl], writes=[rl])
        vop(lambda e, c=c: e.tensor_scalar(lnS[:, c + 4:c + 5], lnS[:, c + 1:c + 2], 1.0 / D, LN_EPS, ALU.mult, ALU.add),
            reads=[rl], writes=[rl])
        vop(lambda e, c=c: e.tensor_tensor(lnS[:, c + 4:c + 5], lnS[:, c + 4:c + 5], lnS[:, c + 3:c + 4], ALU.subtract),
            reads=[rl], writes=[rl])
        sop(lambda e, c=c: e.activation(lnS[:, c + 5:c + 6], lnS[:, c + 4:c + 5], AF.Sqrt),
            reads=[rl], writes=[rl])
        vop(lambda e, c=c: e.reciprocal(lnS[:, c + 6:c + 7], lnS[:, c + 5:c + 6]), reads=[rl], writes=[rl])
        vop(lambda e, c=c: e.tensor_tensor(lnS[:, c + 7:c + 8], lnS[:, c + 4:c + 5], lnS[:, c + 6:c + 7], ALU.mult),
            reads=[rl], writes=[rl])
        vop(lambda e, c=c: e.tensor_tensor(lnS[:, c + 7:c + 8], lnS[:, c + 7:c + 8], lnS[:, c + 6:c + 7], ALU.mult),
            reads=[rl], writes=[rl])
        vop(lambda e, c=c: e.tensor_scalar(lnS[:, c + 7:c + 8], lnS[:, c + 7:c + 8], -0.5, 1.5, ALU.mult, ALU.add),
            reads=[rl], writes=[rl])
        vop(lambda e, c=c: e.tensor_tensor(lnS[:, c + 6:c + 7], lnS[:, c + 6:c + 7], lnS[:, c + 7:c + 8], ALU.mult),
            reads=[rl], writes=[rl])
        vop(lambda e, c=c: e.scalar_tensor_tensor(lnS[:, c + 3:c + 4], lnS[:, c + 2:c + 3], -1.0,
                                                  lnS[:, c + 6:c + 7], ALU.mult, ALU.mult),
            reads=[rl], writes=[rl])
        sop(lambda e, c=c: e.activation(xndst, x_ap, AF.Identity, bias=lnS[:, c + 3:c + 4],
                                        scale=lnS[:, c + 6:c + 7]),
            reads=[rx, rl], writes=[rxn])

    def tm_to_fm(src_ap, rsrc, dsth_ap, dstl_ap, rdst, toff):
        d3h = dsth_ap.rearrange("p (hc t) -> p hc t", hc=HC)
        d3l = dstl_ap.rearrange("p (hc t) -> p hc t", hc=HC)
        for half in range(2):
            bank = half
            for i in range(4):
                hc = half * 4 + i
                top(lambda e, hc=hc, i=i, bank=bank: e.transpose(
                        pb[bank][:, i * 128:(i + 1) * 128],
                        src_ap[:, hc * 128:(hc + 1) * 128], ident_sb),
                    reads=[rsrc, r_ident], writes=[pres[bank]])
            def ev(e, bank=bank, half=half):
                return e.activation(d3h[:, half * 4:(half + 1) * 4, toff:toff + 128],
                                    pb[bank][:, 0:512].rearrange("p (a t) -> p a t", a=4),
                                    AF.Identity)
            sop(ev, reads=[pres[bank]], writes=[rdst])
            def evl(e, bank=bank, half=half):
                return e.scalar_tensor_tensor(
                    d3l[:, half * 4:(half + 1) * 4, toff:toff + 128],
                    pb[bank][:, 0:512].rearrange("p (a t) -> p a t", a=4), 1.0,
                    d3h[:, half * 4:(half + 1) * 4, toff:toff + 128], ALU.mult, ALU.subtract)
            vop(evl, reads=[pres[bank], rdst], writes=[rdst])

    st_toks = []
    for q in range(NQ):
        REG.reset()
        wqB, r_wq = REG.alloc("wqB", [128, 2 * HC * D], BF16)
        wq_h = wqB[:, 0:HC * D]
        wq_l = wqB[:, HC * D:2 * HC * D]
        dma(wqB.rearrange("p (g f) -> p g f", g=2 * HC),
            wqS.rearrange("h (a p) f -> p h a f", p=128).opt(),
            also=[ph0_last['d_ph0a']], writes=[r_wq])
        xa2, r_xa2_ = REG.alloc("xa2", [128, D], F32)
        xb2, r_xb2_ = REG.alloc("xb2", [128, D], F32)
        xc2, r_xc2_ = REG.alloc("xc2", [128, D], F32)
        xd2, r_xd2_ = REG.alloc("xd2", [128, D], F32)
        lnbuf = [(xa, r_xa, xb, r_xb, xc_, r_xc, xd, r_xd),
                 (xa2, r_xa2_, xb2, r_xb2_, xc2, r_xc2_, xd2, r_xd2_)]

        pend_kv = None
        for si in range(8):
            s_tile = 8 * q + si
            par = si % 2
            bxa, brxa, bxb, brxb, bxc, brxc, bxd, brxd = lnbuf[par]
            dma(bxa, x_seq[s_tile], writes=[brxa])
            emit_ln(bxa[:, :], brxa, bxb[:, :], brxb, bxc[:, :], brxc, bxc[:, :], brxc, par)
            tm_to_fm(bxc, brxc, xn2f_h, xn2f_l, r_xn2f, par * 128)
            bank = 2 + (si % 2)
            xn3h = xn2f_h.rearrange("p (hc t) -> p hc t", hc=HC)
            xn3l = xn2f_l.rearrange("p (hc t) -> p hc t", hc=HC)
            wkv3h = wkv_h.rearrange("p (a f) -> p a f", a=HC)
            wkv3l = wkv_l.rearrange("p (a f) -> p a f", a=HC)
            pairs = [(wkv3h[:, hc, :], wkv3l[:, hc, :],
                      xn3h[:, hc, par * 128:par * 128 + 128],
                      xn3l[:, hc, par * 128:par * 128 + 128]) for hc in range(HC)]
            mmgroup3(pb[bank][:, 0:128], bank, pairs, [r_xn2f, r_wkv])

            def mk_kvtail(bank=bank, s_tile=s_tile):
                def kvtail():
                    sop(lambda e, bank=bank: e.activation(kq[:, 0:128], pb[bank][:, 0:128], AF.Square),
                        reads=[pres[bank]], writes=[r_kq])
                    top(lambda e: e.matmul(pb[4][0:2, 0:128], pool2_sb, kq[:, 0:128],
                                           start=True, stop=True),
                        reads=[r_kq, r_pool2], writes=[pres[4]])
                    vop(lambda e: e.tensor_copy(sA[0:1, 0:128], pb[4][0:1, 0:128]),
                        reads=[pres[4]], writes=[r_sA])
                    rsqrt2(sA[0:1, 0:128], sA[0:1, 128:256], 1, r_sA)
                    top(lambda e: e.matmul(pb[5][0:64, 0:128], ones64r_sb, sA[0:1, 128:256],
                                           start=True, stop=True),
                        reads=[r_sA, r_o64], writes=[pres[5]])
                    vop(lambda e: e.tensor_copy(kq[0:64, 128:256], pb[5][0:64, 0:128]),
                        reads=[pres[5]], writes=[r_kq])
                    def kh(e, bank=bank):
                        return e.scalar_tensor_tensor(kq[0:64, 384:512],
                                                      pb[bank][0:64, 0:128], ks_col_sb,
                                                      kq[0:64, 128:256], ALU.mult, ALU.mult)
                    vop(kh, reads=[pres[bank], r_kq, r_ksc], writes=[r_kq])
                    top(lambda e: e.matmul(pb[7][0:128, 0:128], dup64_sb, kq[0:64, 384:512],
                                           start=True, stop=True),
                        reads=[r_kq, r_dup], writes=[pres[7]])
                    vsplit(pb[7][0:128, 0:128], pres[7],
                           kbuf_h[:, s_tile * 128:(s_tile + 1) * 128],
                           kbuf_l[:, s_tile * 128:(s_tile + 1) * 128], r_kbuf)
                    vop(lambda e, bank=bank: e.tensor_copy(kq[0:64, 256:384], pb[bank][64:128, 0:128]),
                        reads=[pres[bank]], writes=[r_kq])
                    top(lambda e: e.transpose(pb[6][0:128, 0:64], kq[0:64, 256:384],
                                              ident_sb[0:64, 0:64]),
                        reads=[r_kq, r_ident], writes=[pres[6]])
                    vsplit(pb[6][0:128, 0:64], pres[6],
                           vbuf_h[:, s_tile * 65:s_tile * 65 + 64],
                           vbuf_l[:, s_tile * 65:s_tile * 65 + 64], r_vbuf)
                return kvtail
            if pend_kv is not None:
                pend_kv()
            pend_kv = mk_kvtail()

        if pend_kv is not None:
            pend_kv()
            pend_kv = None

        for ti in range(QT):
            tile = q * QT + ti
            par = ti % 2
            bxa, brxa, bxb, brxb, bxc, brxc, bxd, brxd = lnbuf[par]
            dma(bxa, x_own[tile], writes=[brxa])
            vop(lambda e, ti=ti, bxa=bxa: e.tensor_copy(acc[:, ti * D:(ti + 1) * D], bxa),
                reads=[brxa], writes=[r_acc])
            emit_ln(bxa[:, :], brxa, bxb[:, :], brxb, bxc[:, :], brxc, bxc[:, :], brxc, par)
            tm_to_fm(bxc, brxc, xnf_h, xnf_l, r_xnf, ti * 128)
            emit_ln(bxc[:, :], brxc, bxb[:, :], brxb, bxa[:, :], brxa, bxd[:, :], brxd, par)
            vop(lambda e, bxd=bxd: e.tensor_tensor(bxd[:, :], bxd[:, :], gamma_sb, ALU.mult),
                reads=[brxd, r_gamma], writes=[brxd])
            tm_to_fm(bxd, brxd, xn2f_h, xn2f_l, r_xn2f, ti * 128)

        xn2f3h = xn2f_h.rearrange("p (hc t) -> p hc t", hc=HC)
        xn2f3l = xn2f_l.rearrange("p (hc t) -> p hc t", hc=HC)
        wq3h = wq_h.rearrange("p (hc i) -> p hc i", hc=HC)
        wq3l = wq_l.rearrange("p (hc i) -> p hc i", hc=HC)
        pend_q = None
        for ic in range(HC):
            bank = 2 + (ic % 2)
            pairs = [(wq3h[:, hc, ic * 128:(ic + 1) * 128], wq3l[:, hc, ic * 128:(ic + 1) * 128],
                      xn2f3h[:, hc, :], xn2f3l[:, hc, :]) for hc in range(HC)]
            mmgroup3(pb[bank][:, 0:512], bank, pairs, [r_xn2f, r_wq])
            sop(lambda e, bank=bank: e.activation(q2, pb[bank][:, 0:512], AF.Square),
                reads=[pres[bank]], writes=[r_q2])
            top(lambda e: e.matmul(pb[4][0:2, 0:512], pool2_sb, q2, start=True, stop=True),
                reads=[r_q2, r_pool2], writes=[pres[4]])
            sr = 32 * (ic % 2)
            vop(lambda e, sr=sr: e.tensor_copy(sA[sr:sr + 2, 0:512], pb[4][0:2, 0:512]),
                reads=[pres[4]], writes=[r_sA])
            rsqrt2(sA[sr:sr + 2, 0:512], sA[sr:sr + 2, 0:512], 2, r_sA, sr=sr)

            def mk_qtail(ic=ic, bank=bank, sr=sr):
                def qtail():
                    top(lambda e: e.matmul(pb[5][0:128, 0:512], hm2_sb[sr:sr + 2, :],
                                           sA[sr:sr + 2, 0:512], start=True, stop=True),
                        reads=[r_sA, r_hm2], writes=[pres[5]])
                    vop(lambda e: e.tensor_copy(rnb, pb[5][:, 0:512]), reads=[pres[5]], writes=[r_rnb])
                    def qh(e, bank=bank):
                        return e.scalar_tensor_tensor(q2, pb[bank][:, 0:512], qs_col_sb, rnb,
                                                      ALU.mult, ALU.mult)
                    vop(qh, reads=[pres[bank], r_rnb, r_qsc], writes=[r_q2])
                    vsplit(q2, r_q2, qhat_h[:, ic * 512:(ic + 1) * 512],
                           qhat_l[:, ic * 512:(ic + 1) * 512], r_qhat)
                return qtail
            if pend_q is not None:
                pend_q()
            pend_q = mk_qtail()
        pend_q()
        pend_q = None

        REG.reset()
        acat, r_acat = REG.alloc("acat", [128, HC * 512], F32)
        acat_h, r_acatP = REG.alloc("acat_h", [128, HC * 512], BF16)
        acat_l, r_acatL = REG.alloc("acat_l", [128, HC * 512], BF16)
        wo0B, rwo0 = REG.alloc("wo0B", [128, 2 * D], BF16)
        wo1B, rwo1 = REG.alloc("wo1B", [128, 2 * D], BF16)
        psbF2 = []
        psbP2 = []
        for i in range(2):
            ap_, _r = REG.alloc(f"psbF2_{i}", [128, 512], F32)
            psbF2.append((ap_, REG.extra()))
            hh_, _r = REG.alloc(f"psbP2h_{i}", [128, 512], BF16)
            ll_, _r = REG.alloc(f"psbP2l_{i}", [128, 512], BF16)
            psbP2.append((hh_, ll_, REG.extra()))
        nk = 8 * q + 8
        pend_eps = []
        for hp in range(HEADS // 2):
            ic = hp
            sides = []
            for s in range(2):
                ro = 64 * s
                sides.append(dict(
                    ro=ro,
                    qh=qhat_h[ro:ro + 64, ic * 512:(ic + 1) * 512],
                    ql=qhat_l[ro:ro + 64, ic * 512:(ic + 1) * 512],
                    sb0=2 * s, ob=4 if s == 0 else 7,
                    pF=psbF if s == 0 else psbF2,
                    pP=psbP if s == 0 else psbP2,
                    pnr=32 * s,
                ))

            def emitQK(S, k):
                sb_ = S["sb0"] + k % 2
                ro = S["ro"]
                for i3, (lh, rh) in enumerate([
                        (kbuf_h[ro:ro + 64, k * 128:(k + 1) * 128], S["qh"]),
                        (kbuf_h[ro:ro + 64, k * 128:(k + 1) * 128], S["ql"]),
                        (kbuf_l[ro:ro + 64, k * 128:(k + 1) * 128], S["qh"])]):
                    top(lambda e, lh=lh, rh=rh, sb_=sb_, st=(i3 == 0), sp=(i3 == 2): e.matmul(
                            pb[sb_][:, 0:512], lh, rh, start=st, stop=sp),
                        reads=[r_kbuf, r_qhat], writes=[pres[sb_]])

            def emitEXP(S, k):
                sb_ = S["sb0"] + k % 2
                pi = k % 2
                pF, r_pF = S["pF"][pi]
                sop(lambda e, sb_=sb_, pF=pF: e.activation(pF, pb[sb_][:, 0:512], AF.Exp,
                                                           bias=negB_sb, scale=SCALE),
                    reads=[pres[sb_], r_negB], writes=[r_pF])
                jj = k // 2
                if q * QT <= jj < (q + 1) * QT:
                    sub = (jj - q * QT) * 128
                    if k % 2 == 0:
                        m_ap, m_res = (maskA0_sb, r_maskA0) if jj == 0 else (maskA_sb, r_maskA)
                    else:
                        m_ap, m_res = maskB_sb, r_maskB
                    vop(lambda e, pF=pF, sub=sub, m_ap=m_ap: e.tensor_tensor(
                            pF[:, sub:sub + 128], pF[:, sub:sub + 128], m_ap, ALU.mult),
                        reads=[r_pF, m_res], writes=[r_pF])
                ph, pl, r_pp = S["pP"][pi]
                vsplit(pF, r_pF, ph, pl, r_pp, on_scalar_hi=True)

            def emitAV(S, k):
                pi = k % 2
                ph, pl, r_pp = S["pP"][pi]
                ob = S["ob"]
                tstart = (max(q * QT, k // 2) - q * QT) * 128
                for i3, (lh, rh) in enumerate([
                        (vbuf_h[:, k * 65:k * 65 + 65], ph),
                        (vbuf_h[:, k * 65:k * 65 + 65], pl),
                        (vbuf_l[:, k * 65:k * 65 + 65], ph)]):
                    def av(e, lh=lh, rh=rh, tstart=tstart, ob=ob,
                           last=(k == nk - 1 and i3 == 2)):
                        return e.matmul(pb[ob][0:65, tstart:512], lh,
                                        rh[:, tstart:512], start=False, stop=last)
                    top(av, reads=[r_pp, r_vbuf], writes=[pres[ob]])

            for i3, (lh, rh) in enumerate([
                    (nkhT4_h, qhat_h[:, ic * 512:(ic + 1) * 512]),
                    (nkhT4_h, qhat_l[:, ic * 512:(ic + 1) * 512]),
                    (nkhT4_l, qhat_h[:, ic * 512:(ic + 1) * 512])]):
                top(lambda e, lh=lh, rh=rh, st=(i3 == 0), sp=(i3 == 2): e.matmul(
                        pb[6][0:34, 0:512], lh, rh, start=st, stop=sp),
                    reads=[r_qhat, r_nkhT], writes=[pres[6]])
            sop(lambda e: e.activation(kq[0:34, 0:512], pb[6][0:34, 0:512], AF.Exp,
                                       bias=negB_sb[0:34, 0:1], scale=SCALE),
                reads=[pres[6], r_negB], writes=[r_kq])
            vsplit(kq[0:2, 0:512], r_kq, pn_h[0:2, :], pn_l[0:2, :], r_pn)
            vsplit(kq[32:34, 0:512], r_kq, pn_h[32:34, :], pn_l[32:34, :], r_pn)

            for S in sides:
                emitQK(S, 0)
            for S in sides:
                emitEXP(S, 0)
                emitQK(S, 1)
            for ep in pend_eps:
                ep()
            pend_eps = []
            for S in sides:
                pnr = S["pnr"]
                ob = S["ob"]
                for i3, (lh, rh) in enumerate([
                        (nva_h[pnr:pnr + 2, :], pn_h[pnr:pnr + 2, :]),
                        (nva_h[pnr:pnr + 2, :], pn_l[pnr:pnr + 2, :]),
                        (nva_l[pnr:pnr + 2, :], pn_h[pnr:pnr + 2, :])]):
                    top(lambda e, lh=lh, rh=rh, ob=ob, st=(i3 == 0): e.matmul(
                            pb[ob][0:65, 0:512], lh, rh, start=st, stop=False),
                        reads=[r_pn, r_nva], writes=[pres[ob]])
            for k in range(nk):
                if k + 2 < nk:
                    for S in sides:
                        emitQK(S, k + 2)
                for S in sides:
                    emitAV(S, k)
                if k + 1 < nk:
                    for S in sides:
                        emitEXP(S, k + 1)

            def mk_ep(ic=ic, S=None, acat=acat, r_acat=r_acat, sel=0):
                ro = S["ro"]
                ob = S["ob"]
                zrow = sC[0:1, 0:512] if sel == 0 else sB[0:1, 0:512]
                r_zr = r_sC if sel == 0 else r_sB
                zb_ = zb if sel == 0 else rnb[0:64, 0:512]
                r_zb_ = r_zb if sel == 0 else r_rnb
                def ep():
                    vop(lambda e: e.tensor_copy(zrow, pb[ob][64:65, 0:512]),
                        reads=[pres[ob]], writes=[r_zr])
                    vop(lambda e: e.reciprocal(zrow, zrow), reads=[r_zr], writes=[r_zr])
                    top(lambda e: e.matmul(pb[5][0:64, 0:512], ones64r_sb, zrow,
                                           start=True, stop=True),
                        reads=[r_zr, r_o64], writes=[pres[5]])
                    vop(lambda e: e.tensor_copy(zb_, pb[5][0:64, 0:512]),
                        reads=[pres[5]], writes=[r_zb_])
                    def an(e):
                        return e.tensor_tensor(acat[ro:ro + 64, ic * 512:(ic + 1) * 512],
                                               pb[ob][0:64, 0:512], zb_, ALU.mult)
                    vop(an, reads=[pres[ob], r_zb_], writes=[r_acat])
                return ep
            pend_eps = [mk_ep(S=sides[0], sel=0), mk_ep(S=sides[1], sel=1)]
        for ep in pend_eps:
            ep()
        pend_eps = []
        for cc in range(HC):
            vsplit(acat[:, cc * 512:(cc + 1) * 512], r_acat,
                   acat_h[:, cc * 512:(cc + 1) * 512], acat_l[:, cc * 512:(cc + 1) * 512],
                   r_acatP, on_scalar_hi=True)

        wop = [(wo0B, rwo0), (wo1B, rwo1)]
        for ic in range(HC):
            woB, rwo = wop[ic % 2]
            woh = woB[:, 0:D]
            wol = woB[:, D:2 * D]
            dma(woB.rearrange("p (h f) -> p h f", h=2),
                woS[:, ic * 128:(ic + 1) * 128, :].rearrange("h p f -> p h f"),
                also=[ph0_last['d_ph0a']], writes=[rwo])
            for t128 in range(4):
                for dhh in range(2):
                    bank = 2 * t128 + dhh
                    ach = acat_h[:, ic * 512 + t128 * 128: ic * 512 + t128 * 128 + 128]
                    acl = acat_l[:, ic * 512 + t128 * 128: ic * 512 + t128 * 128 + 128]
                    for i3, (lh, rh) in enumerate([
                            (ach, woh[:, dhh * 512:(dhh + 1) * 512]),
                            (ach, wol[:, dhh * 512:(dhh + 1) * 512]),
                            (acl, woh[:, dhh * 512:(dhh + 1) * 512])]):
                        def wm(e, lh=lh, rh=rh, bank=bank, st=(ic == 0 and i3 == 0),
                               sp=(ic == HC - 1 and i3 == 2)):
                            return e.matmul(pb[bank][:, 0:512], lh, rh, start=st, stop=sp)
                        top(wm, reads=[r_acatP, rwo], writes=[pres[bank]])
        for t128 in range(4):
            for dhh in range(2):
                bank = 2 * t128 + dhh
                def accadd(e, bank=bank, t128=t128, dhh=dhh):
                    sl = acc[:, t128 * D + dhh * 512: t128 * D + (dhh + 1) * 512]
                    return e.tensor_tensor(sl, sl, pb[bank][:, 0:512], ALU.add)
                vop(accadd, reads=[pres[bank], r_acc], writes=[r_acc])

        gw3h = gwT_h.rearrange("p (hc e) -> p hc e", hc=HC)
        gw3l = gwT_l.rearrange("p (hc e) -> p hc e", hc=HC)
        xnf3h = xnf_h.rearrange("p (hc t) -> p hc t", hc=HC)
        xnf3l = xnf_l.rearrange("p (hc t) -> p hc t", hc=HC)
        for ti in range(QT):
            bank = 6
            pairs = [(xnf3h[:, hc, ti * 128:(ti + 1) * 128], xnf3l[:, hc, ti * 128:(ti + 1) * 128],
                      gw3h[:, hc, :], gw3l[:, hc, :]) for hc in range(HC)]
            mmgroup3(pb[bank][:, 0:NE], bank, pairs, [r_xnf, r_gwT])
            sop(lambda e, bank=bank: e.activation(g32a, pb[bank][:, 0:NE], AF.Exp, scale=-1.0),
                reads=[pres[bank]], writes=[r_g32a])
            vop(lambda e: e.tensor_scalar(g32a, g32a, 1.0, None, ALU.add),
                reads=[r_g32a], writes=[r_g32a])
            vop(lambda e: e.reciprocal(g32a, g32a), reads=[r_g32a], writes=[r_g32a])
            vop(lambda e: e.tensor_tensor(g32b, g32a, biasb_sb, ALU.add),
                reads=[r_g32a, r_biasb], writes=[r_g32b])
            g32b3 = g32b.rearrange("p (g k) -> p g k", g=4)
            g32c3 = g32c.rearrange("p (g k) -> p g k", g=4)
            for gi in range(4):
                vop(lambda e, gi=gi: e.max(out=g8a, in_=g32b[:, gi * 8:(gi + 1) * 8]),
                    reads=[r_g32b], writes=[r_g8a])
                vop(lambda e, gi=gi: e.tensor_tensor(g4a[:, gi:gi + 1], g8a[:, 0:1],
                                                     g8a[:, 1:2], ALU.add),
                    reads=[r_g8a], writes=[r_g4a])
            vop(lambda e: e.tensor_copy(g8pad[:, 0:4], g4a),
                reads=[r_g4a], writes=[r_g8pad])
            vop(lambda e: e.max(out=g8a, in_=g8pad),
                reads=[r_g8pad], writes=[r_g8a])
            vop(lambda e: e.tensor_scalar(g4b, g4a, g8a[:, 1:2], None, ALU.is_ge),
                reads=[r_g4a, r_g8a], writes=[r_g4b])
            vop(lambda e: e.tensor_tensor(g32c3, g32b3, g4b.unsqueeze(2).broadcast_to([128, 4, 8]),
                                          ALU.mult),
                reads=[r_g32b, r_g4b], writes=[r_g32c])
            vop(lambda e: e.max(out=g8a, in_=g32c), reads=[r_g32c], writes=[r_g8a])
            vop(lambda e: e.tensor_scalar(g32c, g32c, g8a[:, 7:8], None, ALU.is_ge),
                reads=[r_g32c, r_g8a], writes=[r_g32c])
            vop(lambda e: e.tensor_tensor(g32c, g32c, g32a, ALU.mult),
                reads=[r_g32c, r_g32a], writes=[r_g32c])
            vop(lambda e: e.tensor_reduce(g1a, g32c, AX.X, ALU.add),
                reads=[r_g32c], writes=[r_g1a])
            vop(lambda e: e.tensor_scalar(g1a, g1a, 1e-20, None, ALU.add),
                reads=[r_g1a], writes=[r_g1a])
            vop(lambda e: e.reciprocal(g1a, g1a), reads=[r_g1a], writes=[r_g1a])
            def wfo(e, ti=ti):
                return e.tensor_scalar(wf[:, ti * NE:(ti + 1) * NE], g32c, g1a, 2.5,
                                       ALU.mult, ALU.mult)
            vop(wfo, reads=[r_g32c, r_g1a], writes=[r_wf])

        REG.reset()
        shp = []
        for i in range(2):
            ap, r = REG.alloc(f"shp{i}", [128, 3 * D], wdt)
            shp.append((ap, (r, REG.extra(f"su{i}"), REG.extra(f"sd{i}"))))
        pend_sh = None
        for jc in range(HC):
            sp_ap, sp_rs = shp[jc % 2]
            sp_rg, sp_ru, sp_rd = sp_rs
            spb = sp_ap.bitcast(BF16)
            def ldg_(e, spb=spb, jc=jc):
                return e.dma_start(
                    out=spb[:, 0:2 * D].rearrange("p (g f) -> p g f", g=2 * HC),
                    in_=sgT[:, :, jc * 128:(jc + 1) * 128].rearrange("h (a p) f -> p h a f", p=128).opt())
            p.op("sync", ldg_, also=[ph0_last['d_ph0a']], writes=[sp_rg], dma=True)
            def ldu_(e, spb=spb, jc=jc):
                return e.dma_start(
                    out=spb[:, 2 * D:4 * D].rearrange("p (g f) -> p g f", g=2 * HC),
                    in_=suT[:, :, jc * 128:(jc + 1) * 128].rearrange("h (a p) f -> p h a f", p=128).opt())
            p.op("sync", ldu_, also=[ph0_last['d_ph0a']], writes=[sp_ru], dma=True)
            def ldd_(e, spb=spb, jc=jc):
                return e.dma_start(
                    out=spb[:, 4 * D:6 * D].rearrange("p (h f) -> p h f", h=2),
                    in_=sdT[:, jc * 128:(jc + 1) * 128, :].rearrange("h p f -> p h f"))
            p.op("sync", ldd_, also=[ph0_last['d_ph0a']], writes=[sp_rd], dma=True)
            sg3h = spb[:, 0:D].rearrange("p (a f) -> p a f", a=HC)
            sg3l = spb[:, D:2 * D].rearrange("p (a f) -> p a f", a=HC)
            su3h = spb[:, 2 * D:3 * D].rearrange("p (a f) -> p a f", a=HC)
            su3l = spb[:, 3 * D:4 * D].rearrange("p (a f) -> p a f", a=HC)
            pairs_g = [(sg3h[:, hc, :], sg3l[:, hc, :], xnf3h[:, hc, :], xnf3l[:, hc, :])
                       for hc in range(HC)]
            pairs_u = [(su3h[:, hc, :], su3l[:, hc, :], xnf3h[:, hc, :], xnf3l[:, hc, :])
                       for hc in range(HC)]
            mmgroup3(pb[0][:, 0:512], 0, pairs_g, [sp_rg, r_xnf])
            mmgroup3(pb[1][:, 0:512], 1, pairs_u, [sp_ru, r_xnf])
            hsj_h, hsj_l, r_hsj = hsjb[jc % 2]
            sop(lambda e: e.activation(gsil, pb[0][:, 0:512], AF.Silu),
                reads=[pres[0]], writes=[r_gsil])
            vop(lambda e: e.tensor_tensor(hsjF, gsil, pb[1][:, 0:512], ALU.mult),
                reads=[r_gsil, pres[1]], writes=[r_hsjF])
            vsplit(hsjF, r_hsjF, hsj_h, hsj_l, r_hsj, on_scalar_hi=True)

            def mkdowns(hsj_h=hsj_h, hsj_l=hsj_l, r_hsj=r_hsj, spb=spb, sp_r=sp_rd):
                def emit():
                    for t128 in range(4):
                        for dhh in range(2):
                            bank = 2 + (t128 * 2 + dhh) % 2
                            hh = hsj_h[:, t128 * 128:(t128 + 1) * 128]
                            hl = hsj_l[:, t128 * 128:(t128 + 1) * 128]
                            dh_ = spb[:, 4 * D + dhh * 512: 4 * D + (dhh + 1) * 512]
                            dl_ = spb[:, 5 * D + dhh * 512: 5 * D + (dhh + 1) * 512]
                            mmgroup3(pb[bank][:, 0:512], bank,
                                     [(hh, hl, dh_, dl_)], [r_hsj, sp_r])
                            def accadd2(e, bank=bank, t128=t128, dhh=dhh):
                                sl = acc[:, t128 * D + dhh * 512: t128 * D + (dhh + 1) * 512]
                                return e.tensor_tensor(sl, sl, pb[bank][:, 0:512], ALU.add)
                            vop(accadd2, reads=[pres[bank], r_acc], writes=[r_acc])
                return emit
            if pend_sh is not None:
                pend_sh()
            pend_sh = mkdowns()

        if pend_sh is not None:
            pend_sh()

        REG.reset()
        ewb = []
        for i in range(2):
            eg_ap, eg_r = REG.alloc(f"ew{i}_eg", [128, HC * 256], wdt)
            eu_ap, eu_r = REG.alloc(f"ew{i}_eu", [128, HC * 256], wdt)
            ed_ap, ed_r = REG.alloc(f"ew{i}_ed", [128, 2 * D], wdt)
            ewb.append((eg_ap, eu_ap, ed_ap, eg_r, eu_r, ed_r, REG.extra(f"edl{i}")))
        pend_ex = None
        for e_i in range(NE):
            for iic in range(2):
                bi = (2 * e_i + iic) % 2
                eg_ap, eu_ap, ed_ap, eg_r, eu_r, ed_r, ed_r2 = ewb[bi]
                egb = eg_ap.bitcast(BF16)
                eub = eu_ap.bitcast(BF16)
                edb = ed_ap.bitcast(BF16)
                EH = HC * 256
                def ldg(e, egb=egb, e_i=e_i, iic=iic):
                    return e.dma_start(
                        out=egb.rearrange("p (g f) -> p g f", g=2 * HC),
                        in_=egT[e_i][:, :, iic * 256:(iic + 1) * 256].rearrange(
                            "h (a p) f -> p h a f", p=128).opt())
                p.op("sync", ldg, also=[ph0_last['d_ph0b']], writes=[eg_r], dma=True)
                def ldu(e, eub=eub, e_i=e_i, iic=iic):
                    return e.dma_start(
                        out=eub.rearrange("p (g f) -> p g f", g=2 * HC),
                        in_=euT[e_i][:, :, iic * 256:(iic + 1) * 256].rearrange(
                            "h (a p) f -> p h a f", p=128).opt())
                p.op("sync", ldu, also=[ph0_last['d_ph0b']], writes=[eu_r], dma=True)
                def ldd_h(e, edb=edb, e_i=e_i, iic=iic):
                    return e.dma_start(
                        out=edb[:, 0:2 * D].rearrange("p (a f) -> p a f", a=2),
                        in_=edT[e_i][0][iic * 256:(iic + 1) * 256, :].rearrange(
                            "(a p) f -> p a f", p=128))
                p.op("sync", ldd_h, also=[ph0_last['d_ph0b']], writes=[ed_r], dma=True)
                def ldd_l(e, edb=edb, e_i=e_i, iic=iic):
                    return e.dma_start(
                        out=edb[:, 2 * D:4 * D].rearrange("p (a f) -> p a f", a=2),
                        in_=edT[e_i][1][iic * 256:(iic + 1) * 256, :].rearrange(
                            "(a p) f -> p a f", p=128))
                p.op("sync", ldd_l, also=[ph0_last['d_ph0b']], writes=[ed_r2], dma=True)
                he_h, he_l, r_he = heb[(2 * e_i + iic) % 2]
                eg3h = egb[:, 0:EH].rearrange("p (hc i) -> p hc i", hc=HC)
                eg3l = egb[:, EH:2 * EH].rearrange("p (hc i) -> p hc i", hc=HC)
                eu3h = eub[:, 0:EH].rearrange("p (hc i) -> p hc i", hc=HC)
                eu3l = eub[:, EH:2 * EH].rearrange("p (hc i) -> p hc i", hc=HC)
                for i2 in range(2):
                    bg = i2 % 2
                    bu = 2 + (i2 % 2)
                    pairs_g = [(eg3h[:, hc, i2 * 128:(i2 + 1) * 128], eg3l[:, hc, i2 * 128:(i2 + 1) * 128],
                                xnf3h[:, hc, :], xnf3l[:, hc, :]) for hc in range(HC)]
                    pairs_u = [(eu3h[:, hc, i2 * 128:(i2 + 1) * 128], eu3l[:, hc, i2 * 128:(i2 + 1) * 128],
                                xnf3h[:, hc, :], xnf3l[:, hc, :]) for hc in range(HC)]
                    mmgroup3(pb[bg][:, 0:512], bg, pairs_g, [eg_r, r_xnf])
                    mmgroup3(pb[bu][:, 0:512], bu, pairs_u, [eu_r, r_xnf])
                    sop(lambda e, bg=bg: e.activation(gsil, pb[bg][:, 0:512], AF.Silu),
                        reads=[pres[bg]], writes=[r_gsil])
                    def hemul(e, bu=bu):
                        return e.tensor_tensor(heF, gsil, pb[bu][:, 0:512], ALU.mult)
                    vop(hemul, reads=[r_gsil, pres[bu]], writes=[r_heF])
                    vsplit(heF, r_heF, he_h[:, i2 * 512:(i2 + 1) * 512],
                           he_l[:, i2 * 512:(i2 + 1) * 512], r_he, on_scalar_hi=True)

                def mk_edowns(he_h=he_h, he_l=he_l, r_he=r_he, edb=edb, ed_r=ed_r,
                              ed_r2=ed_r2, e_i=e_i):
                    def emit():
                        for t128 in range(4):
                            for dhh in range(2):
                                bank = 4 + (t128 * 2 + dhh) % 4
                                pairs = [(he_h[:, i2 * 512 + t128 * 128: i2 * 512 + t128 * 128 + 128],
                                          he_l[:, i2 * 512 + t128 * 128: i2 * 512 + t128 * 128 + 128],
                                          edb[:, i2 * D + dhh * 512: i2 * D + (dhh + 1) * 512],
                                          edb[:, 2 * D + i2 * D + dhh * 512: 2 * D + i2 * D + (dhh + 1) * 512])
                                         for i2 in range(2)]
                                mmgroup3(pb[bank][:, 0:512], bank, pairs, [r_he, ed_r, ed_r2])
                                def accw(e, bank=bank, t128=t128, dhh=dhh, e_i=e_i):
                                    sl = acc[:, t128 * D + dhh * 512: t128 * D + (dhh + 1) * 512]
                                    return e.scalar_tensor_tensor(
                                        sl, pb[bank][:, 0:512],
                                        wf[:, t128 * NE + e_i: t128 * NE + e_i + 1],
                                        sl, ALU.mult, ALU.add)
                                vop(accw, reads=[pres[bank], r_wf, r_acc], writes=[r_acc])
                    return emit
                if pend_ex is not None:
                    pend_ex()
                pend_ex = mk_edowns()

        if pend_ex is not None:
            pend_ex()

        st = dma(out_d[q * QT:(q + 1) * QT].rearrange("t p f -> p t f"),
                 acc.rearrange("p (t f) -> p t f", t=QT),
                 reads=[r_acc], q="gpsimd", dsem="d_out")
        st_toks.append(st)

    dbg_map = dict(
        kbuf=(kbuf_h, r_kbuf), vbuf=(vbuf_h, r_vbuf), negB=(negB_sb, r_negB),
        nkhT=(nkhT4_h, r_nkhT), gwT=(gwT_h, r_gwT), xnf=(xnf_h, r_xnf),
        qhat=(qhat_h, r_qhat), wf=(wf, r_wf),
    )
    for name in debug:
        ap, rr = dbg_map[name]
        dt = nc.dram_tensor("dbg_" + name, list(ap.shape), ap.dtype, kind="ExternalOutput").ap()
        st_toks.append(dma(dt, ap, reads=[rr], q="gpsimd", dsem="d_out"))
    p.op("sync", None, also=st_toks)
    n, nw = p.finalize_and_emit()
    print(f"program: {n} instructions, {nw} waits")
    return nc


def make_inputs(inputs, core):
    b = core // 2
    par = core % 2
    x = np.asarray(inputs["L0_attn_x"])
    xs = np.ascontiguousarray(x[b].reshape(SEQT, 128, D))
    xo = np.ascontiguousarray(xs[par::2])
    qs = np.asarray(inputs["L0_attn_q_scale"])
    ks = np.asarray(inputs["L0_attn_k_scale"])
    nkv = np.asarray(inputs["L0_attn_null_kv"])

    tri = np.zeros((128, 128), np.float32)
    for t in range(128):
        tri[: t + 1, t] = 1.0
    tri0 = tri.copy()
    tri0[0:64, 0:64] = 1.0
    ones = np.ones((128, 128), np.float32)
    zeros = np.zeros((128, 128), np.float32)
    if par == 0:
        maskA, maskA0, maskB = tri, tri0, zeros
    else:
        maskA, maskA0, maskB = ones, ones, tri

    im = {
        "x_own": xo,
        "x_seq": xs,
        "wq": np.asarray(inputs["L0_attn_Wq"]),
        "wkv": np.asarray(inputs["L0_attn_Wkv"]),
        "wout": np.asarray(inputs["L0_attn_Wout"]),
        "gammab": np.broadcast_to(np.asarray(inputs["L0_attn_gamma"])[None, :], (128, D)),
        "qs_col": np.tile(qs, 2)[:, None],
        "ks_col": ks[:, None],
        "qs_row": qs[None, :],
        "ks_row": ks[None, :],
        "ks2": np.broadcast_to(ks[None, :], (2, DH)),
        "nullk": nkv[0],
        "nullv": nkv[1],
        "gate_w": np.asarray(inputs["L0_ffn_gate_w"]),
        "biasb": np.broadcast_to(np.asarray(inputs["L0_ffn_gate_bias"])[None, :], (128, NE)),
        "eg": np.asarray(inputs["L0_ffn_expert_gate"]),
        "eu": np.asarray(inputs["L0_ffn_expert_up"]),
        "ed": np.asarray(inputs["L0_ffn_expert_down"]),
        "sg": np.asarray(inputs["L0_ffn_shared_gate"]),
        "su": np.asarray(inputs["L0_ffn_shared_up"]),
        "sd": np.asarray(inputs["L0_ffn_shared_down"]),
        "ident": np.eye(128, dtype=np.float32),
        "pool2": (np.arange(128)[:, None] // 64 == np.arange(2)[None, :]).astype(np.float32),
        "hm2": np.concatenate([
            (np.arange(128)[None, :] // 64 == np.arange(2)[:, None]).astype(np.float32),
            np.zeros((30, 128), np.float32),
            (np.arange(128)[None, :] // 64 == np.arange(2)[:, None]).astype(np.float32),
        ], 0),
        "ones64r": np.ones((1, 64), np.float32),
        "ones128r": np.ones((1, 128), np.float32),
        "maskA": maskA, "maskA0": maskA0, "maskB": maskB,
        "dup64": (np.arange(128)[None, :] % 64 == np.arange(64)[:, None]).astype(np.float32),
    }
    return {k: np.ascontiguousarray(v, dtype=np.float32) for k, v in im.items()}


def gather_output(results):
    out = np.empty((4, SEQT, 128, D), np.float32)
    for core in range(8):
        b, par = core // 2, core % 2
        out[b, par::2] = results[core]["out"]
    return out.reshape(4, 4096, D)


from concourse.bass_utils import run_bass_kernel_spmd


def kernel(**inputs):
    nc = bass.Bass("TRN2", target_bir_lowering=False, detect_race_conditions=False)
    build(nc, mode="f32")
    in_maps = [make_inputs(inputs, c) for c in range(8)]
    res = run_bass_kernel_spmd(nc, in_maps, core_ids=list(range(8)))
    return gather_output(res.results)


# version 10
# speedup vs baseline: 1.0418x; 1.0418x over previous
import numpy as np
import concourse.bass as bass
import concourse.mybir as mybir

F32 = mybir.dt.float32
BF16 = mybir.dt.bfloat16
AF = mybir.ActivationFunctionType
ALU = mybir.AluOpType
AX = mybir.AxisListType

D = 1024
HEADS = 16
DH = 64
NE = 32
EI = 512
SEQT = 32
OWNT = 16
NQ = 4
QT = 4
TQ = 512
HC = 8
LN_EPS = 1e-5
SCALE = 8.0

ENGINES = ("sync", "tensor", "vector", "scalar", "gpsimd")


class Tok:
    __slots__ = ("eng", "dma", "need", "val", "key")
    def __init__(self, eng, dma):
        self.eng = eng
        self.dma = dma
        self.need = dma
        self.val = None
        self.key = None


class Res:
    __slots__ = ("w", "readers", "init", "semkey")
    def __init__(self, init=()):
        self.w = None
        self.readers = []
        self.init = list(init)
        self.semkey = None
    def rdeps(self):
        return [self.w] if self.w is not None else list(self.init)
    def wdeps(self):
        base = [self.w] if self.w is not None else list(self.init)
        return base + self.readers
    def did_read(self, tok):
        self.readers.append(tok)
    def did_write(self, tok):
        self.w = tok
        self.readers = []
    def drain(self):
        out = list(self.init) + self.readers
        if self.w is not None:
            out.append(self.w)
        return out


class P:
    def __init__(self, nc):
        self.nc = nc
        self.order = []
        self.dsems = set()

    def op(self, eng, fn, reads=(), writes=(), also=(), dma=False, dsem=None):
        deps = []
        for r in reads:
            deps += r.rdeps()
        for w in writes:
            deps += w.wdeps()
        deps += [t for t in also if t is not None]
        tok = Tok(eng, dma)
        if dma:
            if dsem is None:
                assert writes, "dma op needs a target Res or explicit dsem"
                key = getattr(writes[0], "semkey", None)
                if key is None:
                    key = f"dR{len(self.dsems)}"
                    writes[0].semkey = key
                    self.dsems.add(key)
                dsem = key
            tok.key = dsem
        self.order.append((eng, fn, deps, tok))
        for r in reads:
            r.did_read(tok)
        for w in writes:
            w.did_write(tok)
        return tok

    def finalize_and_emit(self):
        nc = self.nc
        PIPELINED = ("vector", "scalar")
        for eng, fn, deps, tok in self.order:
            for d in deps:
                if d.eng != eng or d.dma or eng in PIPELINED:
                    d.need = True
        semnames = set()
        for eng, fn, deps, tok in self.order:
            if tok.need:
                if not tok.dma:
                    tok.key = "c_" + eng
                semnames.add(tok.key)
        sems = {k: nc.alloc_semaphore(name=k) for k in sorted(semnames)}
        cnt = {k: 0 for k in semnames}
        for eng, fn, deps, tok in self.order:
            if tok.need:
                cnt[tok.key] += 16 if tok.dma else 1
                tok.val = cnt[tok.key]
        per_eng = {e: [] for e in ENGINES}
        maxwait = {e: {} for e in ENGINES}
        nwaits = 0
        for i, (eng, fn, deps, tok) in enumerate(self.order):
            waits = []
            for d in deps:
                if d.eng == eng and not d.dma and eng not in PIPELINED:
                    continue
                mw = maxwait[eng]
                if mw.get(d.key, 0) >= d.val:
                    continue
                mw[d.key] = d.val
                waits.append((d.key, d.val))
            nwaits += len(waits)
            per_eng[eng].append((waits, fn, tok))
        with nc.Block() as blk:
            def mk(eng_name):
                def body(eng):
                    for waits, fn, tok in per_eng[eng_name]:
                        for k, v in waits:
                            eng.wait_ge(sems[k], v)
                        if fn is None:
                            continue
                        inst = fn(eng)
                        if tok.need:
                            inst.then_inc(sems[tok.key], 16 if tok.dma else 1)
                return body
            blk.sync(mk("sync"))
            blk.tensor(mk("tensor"))
            blk.vector(mk("vector"))
            blk.scalar(mk("scalar"))
            blk.gpsimd(mk("gpsimd"))
        n_inst = sum(len(v) for v in per_eng.values())
        return n_inst, nwaits


class Arena:
    def __init__(self, nc):
        self.nc = nc
        self.cur = nc.sbuf_base
        self.top = nc.sbuf_top
        self.n = 0

    def alloc_at(self, off, name, shape, dtype):
        self.n += 1
        t = self.nc.alloc_sbuf_tensor_at(f"{name}_{self.n}", list(shape), dtype, offset=off)
        return t.ap()

    def alloc(self, name, shape, dtype):
        per_part = int(np.prod(shape[1:])) * mybir.dt.size(dtype)
        off = (self.cur + 63) & ~63
        if off + per_part > self.top:
            raise AssertionError(f"SBUF overflow: {name} needs {per_part}B at {off}, top={self.top}")
        if per_part >= 1024:
            print(f"  alloc {name}: {per_part}B at {off}")
        self.cur = off + per_part
        return self.alloc_at(off, name, shape, dtype), None

    def reserve(self, nbytes):
        off = (self.cur + 63) & ~63
        assert off + nbytes <= self.top
        self.cur = off + nbytes
        return off


class Region:
    def __init__(self, arena, nbytes):
        self.arena = arena
        self.base = arena.reserve(nbytes)
        self.size = nbytes
        self.cur = self.base
        self.gen = []
        self.pending = []

    def alloc(self, name, shape, dtype):
        per_part = int(np.prod(shape[1:])) * mybir.dt.size(dtype)
        off = (self.cur + 63) & ~63
        assert off + per_part <= self.base + self.size, (
            f"Region overflow: {name} needs {per_part}B at {off - self.base}, size {self.size}")
        self.cur = off + per_part
        ap = self.arena.alloc_at(off, name, shape, dtype)
        r = Res(init=self.pending)
        r.semkey = f"dG{off}"
        self.gen.append(r)
        return ap, r

    def extra(self, key=None):
        r = Res(init=self.pending)
        if key is not None:
            r.semkey = "dX" + key
        self.gen.append(r)
        return r

    def reset(self):
        drains = []
        for r in self.gen:
            drains += r.drain()
        seen = set()
        out = []
        for t in drains:
            if id(t) not in seen:
                seen.add(id(t))
                out.append(t)
        self.pending = out
        self.gen = []
        self.cur = self.base


def build(nc, mode="f32", debug=()):
    assert mode == "f32"
    wdt = F32

    p = P(nc)
    A = Arena(nc)

    def din(name, shape, dt=F32):
        return nc.dram_tensor(name, list(shape), dt, kind="ExternalInput").ap()

    x_own = din("x_own", [OWNT, 128, D])
    x_seq = din("x_seq", [SEQT, 128, D])
    wq_d = din("wq", [D, D])
    wkv_d = din("wkv", [D, 2 * DH])
    wout_d = din("wout", [D, D])
    gammab_d = din("gammab", [128, D])
    qs_col_d = din("qs_col", [128, 1])
    ks_col_d = din("ks_col", [64, 1])
    qs_row_d = din("qs_row", [1, DH])
    ks_row_d = din("ks_row", [1, DH])
    ks2_d = din("ks2", [2, DH])
    nullk_d = din("nullk", [2, DH])
    nullv_d = din("nullv", [2, DH])
    gate_w_d = din("gate_w", [NE, D])
    biasb_d = din("biasb", [128, NE])
    eg_d = din("eg", [NE, EI, D])
    eu_d = din("eu", [NE, EI, D])
    ed_d = din("ed", [NE, D, EI])
    sg_d = din("sg", [D, D])
    su_d = din("su", [D, D])
    sd_d = din("sd", [D, D])
    ident_d = din("ident", [128, 128])
    pool2_d = din("pool2", [128, 2])
    hm2_d = din("hm2", [34, 128])
    ones64r_d = din("ones64r", [1, 64])
    ones128r_d = din("ones128r", [1, 128])
    maskA_d = din("maskA", [128, 128])
    maskA0_d = din("maskA0", [128, 128])
    maskB_d = din("maskB", [128, 128])
    dup64_d = din("dup64", [64, 128])

    out_d = nc.dram_tensor("out", [OWNT, 128, D], F32, kind="ExternalOutput").ap()

    def dpair(name, shape):
        return nc.dram_tensor(name, [2] + list(shape), BF16, kind="Internal").ap()
    egT = nc.dram_tensor("egT", [NE, 2, D, EI], BF16, kind="Internal").ap()
    euT = nc.dram_tensor("euT", [NE, 2, D, EI], BF16, kind="Internal").ap()
    edT = nc.dram_tensor("edT", [NE, 2, EI, D], BF16, kind="Internal").ap()
    sgT = dpair("sgT", [D, D])
    suT = dpair("suT", [D, D])
    sdT = dpair("sdT", [D, D])
    wqS = dpair("wqS", [D, D])
    woS = dpair("woS", [D, D])

    pb = [nc.alloc_psum_tensor(f"pb{i}", [128, 512], F32).ap() for i in range(8)]
    pres = [Res() for _ in range(8)]

    def S(name, shape, dtype=F32):
        ap, _ = A.alloc(name, shape, dtype)
        return ap, Res()

    def SP(name, shape):
        h, _ = A.alloc(name + "h", shape, BF16)
        l, _ = A.alloc(name + "l", shape, BF16)
        return h, l, Res()

    acc, r_acc = S("acc", [128, QT * D])
    xnf_h, xnf_l, r_xnf = SP("xnf", [128, HC * TQ])
    kbuf_h, kbuf_l, r_kbuf = SP("kbuf", [128, SEQT * 128])
    vbuf_h, vbuf_l, r_vbuf = SP("vbuf", [128, SEQT * 65])
    xa, r_xa = S("xa", [128, D])
    xb, r_xb = S("xb", [128, D])
    xc_, r_xc = S("xc", [128, D])
    xd, r_xd = S("xd", [128, D])
    lnS, r_lnS = S("lnS", [128, 32])
    r_lnS2 = Res()
    sA, r_sA = S("sA", [34, 512])
    sB, r_sB = S("sB", [34, 512])
    sC, r_sC = S("sC", [34, 512])
    xn2f_h, xn2f_l, r_xn2f = SP("xn2f", [128, HC * TQ])
    q2, r_q2 = S("q2", [128, 512])
    rnb, r_rnb = S("rnb", [128, 512])
    kq, r_kq = S("kq", [128, 512])
    psbF = [S("psbF0", [128, 512]), S("psbF1", [128, 512])]
    psbP = [SP("psbP0", [128, 512]), SP("psbP1", [128, 512])]
    pn_h, pn_l, r_pn = SP("pn", [34, 512])
    zb, r_zb = S("zb", [64, 512])
    g32a, r_g32a = S("g32a", [128, NE])
    g32b, r_g32b = S("g32b", [128, NE])
    g32c, r_g32c = S("g32c", [128, NE])
    g4a, r_g4a = S("g4a", [128, 4])
    g4b, r_g4b = S("g4b", [128, 4])
    g1a, r_g1a = S("g1a", [128, 1])
    g8a, r_g8a = S("g8a", [128, 8])
    g8pad, r_g8pad = S("g8pad", [128, 8])
    gsil, r_gsil = S("gsil", [128, 512])
    hsjF, r_hsjF = S("hsjF", [128, 512])
    heF, r_heF = hsjF, r_hsjF
    hsjb = [SP("hsj0", [128, 512]), SP("hsj1", [128, 512])]
    heb = [SP("he0", [128, 2 * 512]), SP("he1", [128, 2 * 512])]
    wf, r_wf = S("wf", [128, QT * NE])
    gwT_h, gwT_l, r_gwT = SP("gwT", [128, HC * NE])
    wkv_h, wkv_l, r_wkv = SP("wkv_sb", [128, HC * 2 * DH])
    gamma_sb, r_gamma = S("gamma", [128, D])
    maskA_sb, r_maskA = S("maskA", [128, 128], BF16)
    maskA0_sb, r_maskA0 = S("maskA0", [128, 128], BF16)
    maskB_sb, r_maskB = S("maskB", [128, 128], BF16)
    ident_sb, r_ident = S("ident", [128, 128])
    pool2_sb, r_pool2 = S("pool2", [128, 2])
    hm2_sb, r_hm2 = S("hm2", [34, 128])
    ones64r_sb, r_o64 = S("ones64r", [1, 64])
    ones128r_sb, r_o128 = S("ones128r", [1, 128])
    dup64_sb, r_dup = S("dup64", [64, 128])
    qs_col_sb, r_qsc = S("qs_col", [128, 1])
    ks_col_sb, r_ksc = S("ks_col", [64, 1])
    qs_row_sb, r_qsr = S("qs_row", [1, DH])
    ks_row_sb, r_ksr = S("ks_row", [1, DH])
    ks2_sb, r_ks2 = S("ks2", [2, DH])
    nullk_sb, r_nullk = S("nullk", [2, DH])
    nkhT4_h, nkhT4_l, r_nkhT = SP("nkhT4", [128, 34])
    nva_h, nva_l, r_nva = SP("nva", [34, 65])
    nkh64, r_nkh64 = S("nkh64", [64, 2])
    negB_sb, r_negB = S("negB", [128, 1])
    biasb_sb, r_biasb = S("biasb", [128, NE])

    qhat_h, qhat_l, r_qhat = SP("qhat", [128, HC * TQ])

    REG = Region(A, 48 * 1024)
    print(f"static SBUF: {A.cur - REG.size} B/part + region {REG.size}; top={A.top}")

    def dma(dst, src, reads=(), writes=(), also=(), q="sync", dsem=None):
        return p.op(q, lambda e, d=dst, s=src: e.dma_start(out=d, in_=s),
                    reads=reads, writes=writes, also=also, dma=True, dsem=dsem)

    def vop(fn, reads=(), writes=(), also=()):
        return p.op("vector", fn, reads=reads, writes=writes, also=also)

    def sop(fn, reads=(), writes=(), also=()):
        return p.op("scalar", fn, reads=reads, writes=writes, also=also)

    def top(fn, reads=(), writes=(), also=()):
        return p.op("tensor", fn, reads=reads, writes=writes, also=also)

    def mmgroup(ps_ap, bank_i, pairs, reads):
        n = len(pairs)
        for i, (lh, rh) in enumerate(pairs):
            st = (i == 0)
            sp = (i == n - 1)
            top(lambda e, lh=lh, rh=rh, st=st, sp=sp: e.matmul(ps_ap, lh, rh, start=st, stop=sp),
                reads=list(reads), writes=[pres[bank_i]])

    def mmgroup3(ps_ap, bank_i, pairs4, reads):
        flat = []
        for lh_hi, lh_lo, rh_hi, rh_lo in pairs4:
            flat.append((lh_hi, rh_hi))
            flat.append((lh_hi, rh_lo))
            flat.append((lh_lo, rh_hi))
        n = len(flat)
        for i, (lh, rh) in enumerate(flat):
            st = (i == 0)
            sp = (i == n - 1)
            top(lambda e, lh=lh, rh=rh, st=st, sp=sp: e.matmul(ps_ap, lh, rh, start=st, stop=sp),
                reads=list(reads), writes=[pres[bank_i]])

    def vsplit(src_ap, rsrc, hi_ap, lo_ap, rdst, on_scalar_hi=False):
        if on_scalar_hi:
            sop(lambda e: e.activation(hi_ap, src_ap, AF.Identity), reads=[rsrc], writes=[rdst])
        else:
            vop(lambda e: e.tensor_copy(hi_ap, src_ap), reads=[rsrc], writes=[rdst])
        vop(lambda e: e.scalar_tensor_tensor(lo_ap, src_ap, 1.0, hi_ap, ALU.mult, ALU.subtract),
            reads=[rsrc, rdst], writes=[rdst])

    dma(gamma_sb, gammab_d, writes=[r_gamma])
    dma(maskA_sb, maskA_d, writes=[r_maskA], q="gpsimd")
    dma(maskA0_sb, maskA0_d, writes=[r_maskA0], q="gpsimd")
    dma(maskB_sb, maskB_d, writes=[r_maskB], q="gpsimd")
    dma(ident_sb, ident_d, writes=[r_ident])
    dma(pool2_sb, pool2_d, writes=[r_pool2])
    dma(hm2_sb, hm2_d, writes=[r_hm2])
    dma(ones64r_sb, ones64r_d, writes=[r_o64])
    dma(ones128r_sb, ones128r_d, writes=[r_o128])
    dma(dup64_sb, dup64_d, writes=[r_dup])
    dma(qs_col_sb, qs_col_d, writes=[r_qsc])
    dma(ks_col_sb, ks_col_d, writes=[r_ksc])
    dma(qs_row_sb, qs_row_d, writes=[r_qsr])
    dma(ks_row_sb, ks_row_d, writes=[r_ksr])
    dma(ks2_sb, ks2_d, writes=[r_ks2])
    dma(nullk_sb, nullk_d, writes=[r_nullk])
    nvaF, r_nvaF = S("nvaF", [2, 64])
    dma(nvaF, nullv_d, writes=[r_nvaF])
    dma(biasb_sb, biasb_d, writes=[r_biasb])
    dma(xa[:, 0:HC * 2 * DH].rearrange("p (a f) -> p a f", a=HC),
        wkv_d.rearrange("(a p) f -> p a f", p=128), writes=[r_xa])
    vsplit(xa[:, 0:HC * 2 * DH], r_xa, wkv_h, wkv_l, r_wkv)
    vsplit(nvaF, r_nvaF, nva_h[0:2, 0:64], nva_l[0:2, 0:64], r_nva)
    dma(nva_h[32:34, 0:64], nullv_d, writes=[r_nva], q="gpsimd")
    nvaF2 = kq[32:34, 0:64]
    dma(nvaF2, nullv_d, writes=[r_kq])
    vop(lambda e: e.scalar_tensor_tensor(nva_l[32:34, 0:64], nvaF2, 1.0,
                                         nva_h[32:34, 0:64], ALU.mult, ALU.subtract),
        reads=[r_kq, r_nva], writes=[r_nva])

    vop(lambda e: e.memset(vbuf_h, 1.0), writes=[r_vbuf])
    vop(lambda e: e.memset(vbuf_l, 0.0), writes=[r_vbuf])
    vop(lambda e: e.memset(nva_h[0:2, 64:65], 1.0), writes=[r_nva])
    vop(lambda e: e.memset(nva_l[0:2, 64:65], 0.0), writes=[r_nva])
    vop(lambda e: e.memset(nva_h[32:34, 64:65], 1.0), writes=[r_nva])
    vop(lambda e: e.memset(nva_l[32:34, 64:65], 0.0), writes=[r_nva])
    vop(lambda e: e.memset(g8pad, -1e30), writes=[r_g8pad])

    vop(lambda e: e.tensor_reduce(sA[0:1, 0:1], qs_row_sb, AX.X, ALU.max, apply_absolute_value=True),
        reads=[r_qsr], writes=[r_sA])
    vop(lambda e: e.tensor_reduce(sA[0:1, 1:2], ks_row_sb, AX.X, ALU.max, apply_absolute_value=True),
        reads=[r_ksr], writes=[r_sA])
    vop(lambda e: e.tensor_tensor(sA[0:1, 2:3], sA[0:1, 0:1], sA[0:1, 1:2], ALU.mult),
        reads=[r_sA], writes=[r_sA])
    vop(lambda e: e.tensor_scalar(sA[0:1, 3:4], sA[0:1, 2:3], -8.0, None, ALU.mult),
        reads=[r_sA], writes=[r_sA])
    top(lambda e: e.matmul(pb[0][0:128, 0:1], ones128r_sb, sA[0:1, 3:4], start=True, stop=True),
        reads=[r_sA, r_o128], writes=[pres[0]])
    vop(lambda e: e.tensor_copy(negB_sb, pb[0][0:128, 0:1]), reads=[pres[0]], writes=[r_negB])

    ph0n = []
    ph0o = []
    for i in range(2):
        ap, r = REG.alloc(f"ph0n{i}", [128, 2 * D], F32)
        ph0n.append((ap, r))
        ap, r = REG.alloc(f"ph0o{i}", [128, 2 * D], F32)
        ph0o.append((ap, r))

    jobs = []
    for src2, dstS in [(wq_d, wqS), (wout_d, woS)]:
        for roff in range(0, D, 256):
            jobs.append(("s", src2[roff:roff + 256, :], dstS, roff, 256, D, "d_ph0a"))
    for src2, dstT, Rtot, C in [(sg_d, sgT, D, D), (su_d, suT, D, D), (sd_d, sdT, D, D)]:
        for roff in range(0, Rtot, 256):
            jobs.append(("t", src2[roff:roff + 256, :], dstT, roff, 256, C, "d_ph0a"))
    for e in range(NE):
        for src3, dstT, Rtot, C in [(eg_d, egT, EI, D), (eu_d, euT, EI, D), (ed_d, edT, D, EI)]:
            for roff in range(0, Rtot, 256):
                jobs.append(("t", src3[e][roff:roff + 256, :],
                             (dstT[e][0], dstT[e][1]), roff, 256, C, "d_ph0b"))

    ph0_last = {}
    for ji, (kind, src2d, dst2d, roff, Rr, C, grp) in enumerate(jobs):
        nat, natr = ph0n[ji % 2]
        outb, outr = ph0o[ji % 2]
        nrr = Rr // 128
        ncc = C // 128
        dma(nat[:, 0:nrr * C].rearrange("p (a f) -> p a f", a=nrr),
            src2d.rearrange("(a p) f -> p a f", p=128), writes=[natr])
        obh = outb.bitcast(BF16)
        HB = 2 * C
        if kind == "t":
            for cg in range(0, ncc, 2):
                bank = 6 + (cg // 2) % 2
                for cc in range(cg, min(cg + 2, ncc)):
                    for rc in range(nrr):
                        def tf(e, nat=nat, rc=rc, cc=cc, cg=cg, C=C, bank=bank):
                            col = ((cc - cg) * nrr + rc) * 128
                            return e.transpose(pb[bank][:, col:col + 128],
                                               nat[:, rc * C + cc * 128: rc * C + cc * 128 + 128],
                                               ident_sb)
                        top(tf, reads=[natr, r_ident], writes=[pres[bank]])
                ntile = (min(cg + 2, ncc) - cg) * nrr
                def evh(e, cg=cg, Rr=Rr, bank=bank, ntile=ntile, obh=obh):
                    return e.tensor_copy(obh[:, cg * Rr: cg * Rr + ntile * 128],
                                         pb[bank][:, 0:ntile * 128])
                vop(evh, reads=[pres[bank]], writes=[outr])
                def evl(e, cg=cg, Rr=Rr, bank=bank, ntile=ntile, obh=obh, HB=HB):
                    return e.scalar_tensor_tensor(
                        obh[:, HB + cg * Rr: HB + cg * Rr + ntile * 128],
                        pb[bank][:, 0:ntile * 128], 1.0,
                        obh[:, cg * Rr: cg * Rr + ntile * 128], ALU.mult, ALU.subtract)
                vop(evl, reads=[pres[bank], outr], writes=[outr])
        else:
            def evh2(e, nat=nat, nrr=nrr, C=C, obh=obh):
                return e.tensor_copy(obh[:, 0:nrr * C], nat[:, 0:nrr * C])
            vop(evh2, reads=[natr], writes=[outr])
            def evl2(e, nat=nat, nrr=nrr, C=C, obh=obh, HB=HB):
                return e.scalar_tensor_tensor(obh[:, HB:HB + nrr * C], nat[:, 0:nrr * C],
                                              1.0, obh[:, 0:nrr * C], ALU.mult, ALU.subtract)
            vop(evl2, reads=[natr, outr], writes=[outr])
        if kind == "t":
            dsth, dstl = dst2d[0], dst2d[1]
            ph0_last.__setitem__(grp, dma(dsth[:, roff:roff + Rr].rearrange("(a p) f -> p a f", p=128),
                obh[:, 0:ncc * Rr].rearrange("p (a f) -> p a f", a=ncc),
                reads=[outr], q="gpsimd", dsem=grp))
            ph0_last.__setitem__(grp, dma(dstl[:, roff:roff + Rr].rearrange("(a p) f -> p a f", p=128),
                obh[:, HB:HB + ncc * Rr].rearrange("p (a f) -> p a f", a=ncc),
                reads=[outr], q="gpsimd", dsem=grp))
        else:
            dsth, dstl = dst2d[0], dst2d[1]
            ph0_last.__setitem__(grp, dma(dsth[roff:roff + Rr, :].rearrange("(a p) f -> p a f", p=128),
                obh[:, 0:nrr * C].rearrange("p (a f) -> p a f", a=nrr),
                reads=[outr], q="gpsimd", dsem=grp))
            ph0_last.__setitem__(grp, dma(dstl[roff:roff + Rr, :].rearrange("(a p) f -> p a f", p=128),
                obh[:, HB:HB + nrr * C].rearrange("p (a f) -> p a f", a=nrr),
                reads=[outr], q="gpsimd", dsem=grp))

    def rsqrt2(src_ap, dst_ap, rows, rr, sr=0):
        w = src_ap.shape[1]
        b_ = sB[sr:sr + rows, 0:w]
        c_ = sC[sr:sr + rows, 0:w]
        vop(lambda e: e.tensor_scalar_max(b_, src_ap, 1e-24), reads=[rr], writes=[r_sB])
        sop(lambda e: e.activation(c_, b_, AF.Sqrt), reads=[r_sB], writes=[r_sC])
        vop(lambda e: e.reciprocal(dst_ap, c_), reads=[r_sC], writes=[rr])
        vop(lambda e: e.tensor_tensor(c_, b_, dst_ap, ALU.mult), reads=[r_sB, rr], writes=[r_sC])
        vop(lambda e: e.tensor_tensor(c_, c_, dst_ap, ALU.mult), reads=[rr], writes=[r_sC])
        vop(lambda e: e.tensor_scalar(c_, c_, -0.5, 1.5, ALU.mult, ALU.add), reads=[r_sC], writes=[r_sC])
        vop(lambda e: e.tensor_tensor(dst_ap, dst_ap, c_, ALU.mult), reads=[r_sC, rr], writes=[rr])

    sop(lambda e: e.activation(sA[0:2, 64:128], nullk_sb, AF.Square),
        reads=[r_nullk], writes=[r_sA])
    vop(lambda e: e.tensor_reduce(sA[0:2, 128:129], sA[0:2, 64:128], AX.X, ALU.add),
        reads=[r_sA], writes=[r_sA])
    rsqrt2(sA[0:2, 128:129], sA[0:2, 129:130], 2, r_sA)
    vop(lambda e: e.scalar_tensor_tensor(sA[0:2, 192:256], nullk_sb, sA[0:2, 129:130], ks2_sb,
                                         ALU.mult, ALU.mult),
        reads=[r_sA, r_nullk, r_ks2], writes=[r_sA])
    top(lambda e: e.transpose(pb[0][0:64, 0:2], sA[0:2, 192:256], ident_sb[0:2, 0:2]),
        reads=[r_sA, r_ident], writes=[pres[0]])
    vop(lambda e: e.tensor_copy(nkh64, pb[0][0:64, 0:2]), reads=[pres[0]], writes=[r_nkh64])
    top(lambda e: e.matmul(pb[1][0:128, 0:2], dup64_sb, nkh64, start=True, stop=True),
        reads=[r_nkh64, r_dup], writes=[pres[1]])
    vop(lambda e: e.memset(nkhT4_h, 0.0), writes=[r_nkhT])
    vop(lambda e: e.memset(nkhT4_l, 0.0), writes=[r_nkhT])
    vsplit(pb[1][0:64, 0:2], pres[1], nkhT4_h[0:64, 0:2], nkhT4_l[0:64, 0:2], r_nkhT)
    vsplit(pb[1][64:128, 0:2], pres[1], nkhT4_h[64:128, 32:34], nkhT4_l[64:128, 32:34], r_nkhT)

    gwn, r_gwn = REG.alloc("gwn", [32, D], F32)
    dma(gwn, gate_w_d, writes=[r_gwn])
    for hc in range(HC):
        top(lambda e, hc=hc: e.transpose(pb[1][0:128, hc * NE:(hc + 1) * NE],
                                         gwn[:, hc * 128:(hc + 1) * 128], ident_sb[0:32, 0:32]),
            reads=[r_gwn, r_ident], writes=[pres[1]])
    vsplit(pb[1][:, 0:HC * NE], pres[1], gwT_h, gwT_l, r_gwT)

    def emit_ln(x_ap, rx, xcdst, rxc, sqdst, rsq, xndst, rxn, par=0):
        c = 16 * par
        rl = r_lnS if par == 0 else r_lnS2
        vop(lambda e, c=c: e.tensor_reduce(lnS[:, c:c + 1], x_ap, AX.X, ALU.add),
            reads=[rx], writes=[rl])
        sop(lambda e, c=c: e.activation(sqdst, x_ap, AF.Square, accum_out=lnS[:, c + 1:c + 2]),
            reads=[rx], writes=[rsq, rl])
        vop(lambda e, c=c: e.tensor_scalar(lnS[:, c + 2:c + 3], lnS[:, c:c + 1], 1.0 / D, None, ALU.mult),
            reads=[rl], writes=[rl])
        vop(lambda e, c=c: e.tensor_tensor(lnS[:, c + 3:c + 4], lnS[:, c + 2:c + 3], lnS[:, c + 2:c + 3], ALU.mult),
            reads=[rl], writes=[rl])
        vop(lambda e, c=c: e.tensor_scalar(lnS[:, c + 4:c + 5], lnS[:, c + 1:c + 2], 1.0 / D, LN_EPS, ALU.mult, ALU.add),
            reads=[rl], writes=[rl])
        vop(lambda e, c=c: e.tensor_tensor(lnS[:, c + 4:c + 5], lnS[:, c + 4:c + 5], lnS[:, c + 3:c + 4], ALU.subtract),
            reads=[rl], writes=[rl])
        sop(lambda e, c=c: e.activation(lnS[:, c + 5:c + 6], lnS[:, c + 4:c + 5], AF.Sqrt),
            reads=[rl], writes=[rl])
        vop(lambda e, c=c: e.reciprocal(lnS[:, c + 6:c + 7], lnS[:, c + 5:c + 6]), reads=[rl], writes=[rl])
        vop(lambda e, c=c: e.tensor_tensor(lnS[:, c + 7:c + 8], lnS[:, c + 4:c + 5], lnS[:, c + 6:c + 7], ALU.mult),
            reads=[rl], writes=[rl])
        vop(lambda e, c=c: e.tensor_tensor(lnS[:, c + 7:c + 8], lnS[:, c + 7:c + 8], lnS[:, c + 6:c + 7], ALU.mult),
            reads=[rl], writes=[rl])
        vop(lambda e, c=c: e.tensor_scalar(lnS[:, c + 7:c + 8], lnS[:, c + 7:c + 8], -0.5, 1.5, ALU.mult, ALU.add),
            reads=[rl], writes=[rl])
        vop(lambda e, c=c: e.tensor_tensor(lnS[:, c + 6:c + 7], lnS[:, c + 6:c + 7], lnS[:, c + 7:c + 8], ALU.mult),
            reads=[rl], writes=[rl])
        vop(lambda e, c=c: e.scalar_tensor_tensor(lnS[:, c + 3:c + 4], lnS[:, c + 2:c + 3], -1.0,
                                                  lnS[:, c + 6:c + 7], ALU.mult, ALU.mult),
            reads=[rl], writes=[rl])
        sop(lambda e, c=c: e.activation(xndst, x_ap, AF.Identity, bias=lnS[:, c + 3:c + 4],
                                        scale=lnS[:, c + 6:c + 7]),
            reads=[rx, rl], writes=[rxn])

    def tm_to_fm(src_ap, rsrc, dsth_ap, dstl_ap, rdst, toff, nb=2):
        d3h = dsth_ap.rearrange("p (hc t) -> p hc t", hc=HC)
        d3l = dstl_ap.rearrange("p (hc t) -> p hc t", hc=HC)
        for half in range(2):
            bank = (tm_to_fm.cnt % nb) if nb > 2 else half
            tm_to_fm.cnt += 1
            for i in range(4):
                hc = half * 4 + i
                top(lambda e, hc=hc, i=i, bank=bank: e.transpose(
                        pb[bank][:, i * 128:(i + 1) * 128],
                        src_ap[:, hc * 128:(hc + 1) * 128], ident_sb),
                    reads=[rsrc, r_ident], writes=[pres[bank]])
            def ev(e, bank=bank, half=half):
                return e.activation(d3h[:, half * 4:(half + 1) * 4, toff:toff + 128],
                                    pb[bank][:, 0:512].rearrange("p (a t) -> p a t", a=4),
                                    AF.Identity)
            sop(ev, reads=[pres[bank]], writes=[rdst])
            def evl(e, bank=bank, half=half):
                return e.scalar_tensor_tensor(
                    d3l[:, half * 4:(half + 1) * 4, toff:toff + 128],
                    pb[bank][:, 0:512].rearrange("p (a t) -> p a t", a=4), 1.0,
                    d3h[:, half * 4:(half + 1) * 4, toff:toff + 128], ALU.mult, ALU.subtract)
            vop(evl, reads=[pres[bank], rdst], writes=[rdst])

    tm_to_fm.cnt = 0

    st_toks = []
    for q in range(NQ):
        REG.reset()
        wqB, r_wq = REG.alloc("wqB", [128, 2 * HC * D], BF16)
        wq_h = wqB[:, 0:HC * D]
        wq_l = wqB[:, HC * D:2 * HC * D]
        dma(wqB.rearrange("p (g f) -> p g f", g=2 * HC),
            wqS.rearrange("h (a p) f -> p h a f", p=128).opt(),
            also=[ph0_last['d_ph0a']], writes=[r_wq])
        xa2, r_xa2_ = REG.alloc("xa2", [128, D], F32)
        xb2, r_xb2_ = REG.alloc("xb2", [128, D], F32)
        xc2, r_xc2_ = REG.alloc("xc2", [128, D], F32)
        xd2, r_xd2_ = REG.alloc("xd2", [128, D], F32)
        lnbuf = [(xa, r_xa, xb, r_xb, xc_, r_xc, xd, r_xd),
                 (xa2, r_xa2_, xb2, r_xb2_, xc2, r_xc2_, xd2, r_xd2_)]

        pend_kv = None
        for si in range(8):
            s_tile = 8 * q + si
            par = si % 2
            bxa, brxa, bxb, brxb, bxc, brxc, bxd, brxd = lnbuf[par]
            dma(bxa, x_seq[s_tile], writes=[brxa])
            emit_ln(bxa[:, :], brxa, bxb[:, :], brxb, bxc[:, :], brxc, bxc[:, :], brxc, par)
            tm_to_fm(bxc, brxc, xn2f_h, xn2f_l, r_xn2f, par * 128)
            bank = 2 + (si % 2)
            xn3h = xn2f_h.rearrange("p (hc t) -> p hc t", hc=HC)
            xn3l = xn2f_l.rearrange("p (hc t) -> p hc t", hc=HC)
            wkv3h = wkv_h.rearrange("p (a f) -> p a f", a=HC)
            wkv3l = wkv_l.rearrange("p (a f) -> p a f", a=HC)
            pairs = [(wkv3h[:, hc, :], wkv3l[:, hc, :],
                      xn3h[:, hc, par * 128:par * 128 + 128],
                      xn3l[:, hc, par * 128:par * 128 + 128]) for hc in range(HC)]
            mmgroup3(pb[bank][:, 0:128], bank, pairs, [r_xn2f, r_wkv])

            def mk_kvtail(bank=bank, s_tile=s_tile):
                def kvtail():
                    sop(lambda e, bank=bank: e.activation(kq[:, 0:128], pb[bank][:, 0:128], AF.Square),
                        reads=[pres[bank]], writes=[r_kq])
                    top(lambda e: e.matmul(pb[4][0:2, 0:128], pool2_sb, kq[:, 0:128],
                                           start=True, stop=True),
                        reads=[r_kq, r_pool2], writes=[pres[4]])
                    vop(lambda e: e.tensor_copy(sA[0:1, 0:128], pb[4][0:1, 0:128]),
                        reads=[pres[4]], writes=[r_sA])
                    rsqrt2(sA[0:1, 0:128], sA[0:1, 128:256], 1, r_sA)
                    top(lambda e: e.matmul(pb[5][0:64, 0:128], ones64r_sb, sA[0:1, 128:256],
                                           start=True, stop=True),
                        reads=[r_sA, r_o64], writes=[pres[5]])
                    vop(lambda e: e.tensor_copy(kq[0:64, 128:256], pb[5][0:64, 0:128]),
                        reads=[pres[5]], writes=[r_kq])
                    def kh(e, bank=bank):
                        return e.scalar_tensor_tensor(kq[0:64, 384:512],
                                                      pb[bank][0:64, 0:128], ks_col_sb,
                                                      kq[0:64, 128:256], ALU.mult, ALU.mult)
                    vop(kh, reads=[pres[bank], r_kq, r_ksc], writes=[r_kq])
                    top(lambda e: e.matmul(pb[7][0:128, 0:128], dup64_sb, kq[0:64, 384:512],
                                           start=True, stop=True),
                        reads=[r_kq, r_dup], writes=[pres[7]])
                    vsplit(pb[7][0:128, 0:128], pres[7],
                           kbuf_h[:, s_tile * 128:(s_tile + 1) * 128],
                           kbuf_l[:, s_tile * 128:(s_tile + 1) * 128], r_kbuf)
                    vop(lambda e, bank=bank: e.tensor_copy(kq[0:64, 256:384], pb[bank][64:128, 0:128]),
                        reads=[pres[bank]], writes=[r_kq])
                    top(lambda e: e.transpose(pb[6][0:128, 0:64], kq[0:64, 256:384],
                                              ident_sb[0:64, 0:64]),
                        reads=[r_kq, r_ident], writes=[pres[6]])
                    vsplit(pb[6][0:128, 0:64], pres[6],
                           vbuf_h[:, s_tile * 65:s_tile * 65 + 64],
                           vbuf_l[:, s_tile * 65:s_tile * 65 + 64], r_vbuf)
                return kvtail
            if pend_kv is not None:
                pend_kv()
            pend_kv = mk_kvtail()

        if pend_kv is not None:
            pend_kv()
            pend_kv = None

        for ti in range(QT):
            tile = q * QT + ti
            par = ti % 2
            bxa, brxa, bxb, brxb, bxc, brxc, bxd, brxd = lnbuf[par]
            dma(bxa, x_own[tile], writes=[brxa])
            vop(lambda e, ti=ti, bxa=bxa: e.tensor_copy(acc[:, ti * D:(ti + 1) * D], bxa),
                reads=[brxa], writes=[r_acc])
            emit_ln(bxa[:, :], brxa, bxb[:, :], brxb, bxc[:, :], brxc, bxc[:, :], brxc, par)
            tm_to_fm(bxc, brxc, xnf_h, xnf_l, r_xnf, ti * 128, nb=4)
            emit_ln(bxc[:, :], brxc, bxb[:, :], brxb, bxa[:, :], brxa, bxd[:, :], brxd, par)
            vop(lambda e, bxd=bxd: e.tensor_tensor(bxd[:, :], bxd[:, :], gamma_sb, ALU.mult),
                reads=[brxd, r_gamma], writes=[brxd])
            tm_to_fm(bxd, brxd, xn2f_h, xn2f_l, r_xn2f, ti * 128, nb=4)

        xn2f3h = xn2f_h.rearrange("p (hc t) -> p hc t", hc=HC)
        xn2f3l = xn2f_l.rearrange("p (hc t) -> p hc t", hc=HC)
        wq3h = wq_h.rearrange("p (hc i) -> p hc i", hc=HC)
        wq3l = wq_l.rearrange("p (hc i) -> p hc i", hc=HC)
        pend_q = None
        for ic in range(HC):
            bank = 2 + (ic % 2)
            pairs = [(wq3h[:, hc, ic * 128:(ic + 1) * 128], wq3l[:, hc, ic * 128:(ic + 1) * 128],
                      xn2f3h[:, hc, :], xn2f3l[:, hc, :]) for hc in range(HC)]
            mmgroup3(pb[bank][:, 0:512], bank, pairs, [r_xn2f, r_wq])
            sop(lambda e, bank=bank: e.activation(q2, pb[bank][:, 0:512], AF.Square),
                reads=[pres[bank]], writes=[r_q2])
            top(lambda e: e.matmul(pb[4][0:2, 0:512], pool2_sb, q2, start=True, stop=True),
                reads=[r_q2, r_pool2], writes=[pres[4]])
            sr = 32 * (ic % 2)
            vop(lambda e, sr=sr: e.tensor_copy(sA[sr:sr + 2, 0:512], pb[4][0:2, 0:512]),
                reads=[pres[4]], writes=[r_sA])
            rsqrt2(sA[sr:sr + 2, 0:512], sA[sr:sr + 2, 0:512], 2, r_sA, sr=sr)

            def mk_qtail(ic=ic, bank=bank, sr=sr):
                def qtail():
                    top(lambda e: e.matmul(pb[5][0:128, 0:512], hm2_sb[sr:sr + 2, :],
                                           sA[sr:sr + 2, 0:512], start=True, stop=True),
                        reads=[r_sA, r_hm2], writes=[pres[5]])
                    vop(lambda e: e.tensor_copy(rnb, pb[5][:, 0:512]), reads=[pres[5]], writes=[r_rnb])
                    def qh(e, bank=bank):
                        return e.scalar_tensor_tensor(q2, pb[bank][:, 0:512], qs_col_sb, rnb,
                                                      ALU.mult, ALU.mult)
                    vop(qh, reads=[pres[bank], r_rnb, r_qsc], writes=[r_q2])
                    vsplit(q2, r_q2, qhat_h[:, ic * 512:(ic + 1) * 512],
                           qhat_l[:, ic * 512:(ic + 1) * 512], r_qhat)
                return qtail
            if pend_q is not None:
                pend_q()
            pend_q = mk_qtail()
        pend_q()
        pend_q = None

        REG.reset()
        acat, r_acat = REG.alloc("acat", [128, HC * 512], F32)
        acat_h, r_acatP = REG.alloc("acat_h", [128, HC * 512], BF16)
        acat_l, r_acatL = REG.alloc("acat_l", [128, HC * 512], BF16)
        wo0B, rwo0 = REG.alloc("wo0B", [128, 2 * D], BF16)
        wo1B, rwo1 = REG.alloc("wo1B", [128, 2 * D], BF16)
        psbF2 = []
        psbP2 = []
        for i in range(2):
            ap_, _r = REG.alloc(f"psbF2_{i}", [128, 512], F32)
            psbF2.append((ap_, REG.extra()))
            hh_, _r = REG.alloc(f"psbP2h_{i}", [128, 512], BF16)
            ll_, _r = REG.alloc(f"psbP2l_{i}", [128, 512], BF16)
            psbP2.append((hh_, ll_, REG.extra()))
        nk = 8 * q + 8
        pend_eps = []
        for hp in range(HEADS // 2):
            ic = hp
            sides = []
            for s in range(2):
                ro = 64 * s
                sides.append(dict(
                    ro=ro,
                    qh=qhat_h[ro:ro + 64, ic * 512:(ic + 1) * 512],
                    ql=qhat_l[ro:ro + 64, ic * 512:(ic + 1) * 512],
                    sb0=2 * s, ob=4 if s == 0 else 7,
                    pF=psbF if s == 0 else psbF2,
                    pP=psbP if s == 0 else psbP2,
                    pnr=32 * s,
                ))

            def emitQK(S, k):
                sb_ = S["sb0"] + k % 2
                ro = S["ro"]
                for i3, (lh, rh) in enumerate([
                        (kbuf_h[ro:ro + 64, k * 128:(k + 1) * 128], S["qh"]),
                        (kbuf_h[ro:ro + 64, k * 128:(k + 1) * 128], S["ql"]),
                        (kbuf_l[ro:ro + 64, k * 128:(k + 1) * 128], S["qh"])]):
                    top(lambda e, lh=lh, rh=rh, sb_=sb_, st=(i3 == 0), sp=(i3 == 2): e.matmul(
                            pb[sb_][:, 0:512], lh, rh, start=st, stop=sp),
                        reads=[r_kbuf, r_qhat], writes=[pres[sb_]])

            def emitEXP(S, k):
                sb_ = S["sb0"] + k % 2
                pi = k % 2
                pF, r_pF = S["pF"][pi]
                sop(lambda e, sb_=sb_, pF=pF: e.activation(pF, pb[sb_][:, 0:512], AF.Exp,
                                                           bias=negB_sb, scale=SCALE),
                    reads=[pres[sb_], r_negB], writes=[r_pF])
                jj = k // 2
                if q * QT <= jj < (q + 1) * QT:
                    sub = (jj - q * QT) * 128
                    if k % 2 == 0:
                        m_ap, m_res = (maskA0_sb, r_maskA0) if jj == 0 else (maskA_sb, r_maskA)
                    else:
                        m_ap, m_res = maskB_sb, r_maskB
                    vop(lambda e, pF=pF, sub=sub, m_ap=m_ap: e.tensor_tensor(
                            pF[:, sub:sub + 128], pF[:, sub:sub + 128], m_ap, ALU.mult),
                        reads=[r_pF, m_res], writes=[r_pF])
                ph, pl, r_pp = S["pP"][pi]
                vsplit(pF, r_pF, ph, pl, r_pp, on_scalar_hi=True)

            def emitAV(S, k):
                pi = k % 2
                ph, pl, r_pp = S["pP"][pi]
                ob = S["ob"]
                tstart = (max(q * QT, k // 2) - q * QT) * 128
                for i3, (lh, rh) in enumerate([
                        (vbuf_h[:, k * 65:k * 65 + 65], ph),
                        (vbuf_h[:, k * 65:k * 65 + 65], pl),
                        (vbuf_l[:, k * 65:k * 65 + 65], ph)]):
                    def av(e, lh=lh, rh=rh, tstart=tstart, ob=ob,
                           last=(k == nk - 1 and i3 == 2)):
                        return e.matmul(pb[ob][0:65, tstart:512], lh,
                                        rh[:, tstart:512], start=False, stop=last)
                    top(av, reads=[r_pp, r_vbuf], writes=[pres[ob]])

            for i3, (lh, rh) in enumerate([
                    (nkhT4_h, qhat_h[:, ic * 512:(ic + 1) * 512]),
                    (nkhT4_h, qhat_l[:, ic * 512:(ic + 1) * 512]),
                    (nkhT4_l, qhat_h[:, ic * 512:(ic + 1) * 512])]):
                top(lambda e, lh=lh, rh=rh, st=(i3 == 0), sp=(i3 == 2): e.matmul(
                        pb[6][0:34, 0:512], lh, rh, start=st, stop=sp),
                    reads=[r_qhat, r_nkhT], writes=[pres[6]])
            sop(lambda e: e.activation(kq[0:34, 0:512], pb[6][0:34, 0:512], AF.Exp,
                                       bias=negB_sb[0:34, 0:1], scale=SCALE),
                reads=[pres[6], r_negB], writes=[r_kq])
            vsplit(kq[0:2, 0:512], r_kq, pn_h[0:2, :], pn_l[0:2, :], r_pn)
            vsplit(kq[32:34, 0:512], r_kq, pn_h[32:34, :], pn_l[32:34, :], r_pn)

            for S in sides:
                emitQK(S, 0)
            for S in sides:
                emitEXP(S, 0)
                emitQK(S, 1)
            for ep in pend_eps:
                ep()
            pend_eps = []
            for S in sides:
                pnr = S["pnr"]
                ob = S["ob"]
                for i3, (lh, rh) in enumerate([
                        (nva_h[pnr:pnr + 2, :], pn_h[pnr:pnr + 2, :]),
                        (nva_h[pnr:pnr + 2, :], pn_l[pnr:pnr + 2, :]),
                        (nva_l[pnr:pnr + 2, :], pn_h[pnr:pnr + 2, :])]):
                    top(lambda e, lh=lh, rh=rh, ob=ob, st=(i3 == 0): e.matmul(
                            pb[ob][0:65, 0:512], lh, rh, start=st, stop=False),
                        reads=[r_pn, r_nva], writes=[pres[ob]])
            for k in range(nk):
                if k + 2 < nk:
                    for S in sides:
                        emitQK(S, k + 2)
                for S in sides:
                    emitAV(S, k)
                if k + 1 < nk:
                    for S in sides:
                        emitEXP(S, k + 1)

            def mk_ep(ic=ic, S=None, acat=acat, r_acat=r_acat, sel=0):
                ro = S["ro"]
                ob = S["ob"]
                zrow = sC[0:1, 0:512] if sel == 0 else sB[0:1, 0:512]
                r_zr = r_sC if sel == 0 else r_sB
                zb_ = zb if sel == 0 else rnb[0:64, 0:512]
                r_zb_ = r_zb if sel == 0 else r_rnb
                def ep():
                    vop(lambda e: e.tensor_copy(zrow, pb[ob][64:65, 0:512]),
                        reads=[pres[ob]], writes=[r_zr])
                    vop(lambda e: e.reciprocal(zrow, zrow), reads=[r_zr], writes=[r_zr])
                    top(lambda e: e.matmul(pb[5][0:64, 0:512], ones64r_sb, zrow,
                                           start=True, stop=True),
                        reads=[r_zr, r_o64], writes=[pres[5]])
                    vop(lambda e: e.tensor_copy(zb_, pb[5][0:64, 0:512]),
                        reads=[pres[5]], writes=[r_zb_])
                    def an(e):
                        return e.tensor_tensor(acat[ro:ro + 64, ic * 512:(ic + 1) * 512],
                                               pb[ob][0:64, 0:512], zb_, ALU.mult)
                    vop(an, reads=[pres[ob], r_zb_], writes=[r_acat])
                return ep
            pend_eps = [mk_ep(S=sides[0], sel=0), mk_ep(S=sides[1], sel=1)]
        for ep in pend_eps:
            ep()
        pend_eps = []
        for cc in range(HC):
            vsplit(acat[:, cc * 512:(cc + 1) * 512], r_acat,
                   acat_h[:, cc * 512:(cc + 1) * 512], acat_l[:, cc * 512:(cc + 1) * 512],
                   r_acatP, on_scalar_hi=True)

        wop = [(wo0B, rwo0), (wo1B, rwo1)]
        for ic in range(HC):
            woB, rwo = wop[ic % 2]
            woh = woB[:, 0:D]
            wol = woB[:, D:2 * D]
            dma(woB.rearrange("p (h f) -> p h f", h=2),
                woS[:, ic * 128:(ic + 1) * 128, :].rearrange("h p f -> p h f"),
                also=[ph0_last['d_ph0a']], writes=[rwo])
            for t128 in range(4):
                for dhh in range(2):
                    bank = 2 * t128 + dhh
                    ach = acat_h[:, ic * 512 + t128 * 128: ic * 512 + t128 * 128 + 128]
                    acl = acat_l[:, ic * 512 + t128 * 128: ic * 512 + t128 * 128 + 128]
                    for i3, (lh, rh) in enumerate([
                            (ach, woh[:, dhh * 512:(dhh + 1) * 512]),
                            (ach, wol[:, dhh * 512:(dhh + 1) * 512]),
                            (acl, woh[:, dhh * 512:(dhh + 1) * 512])]):
                        def wm(e, lh=lh, rh=rh, bank=bank, st=(ic == 0 and i3 == 0),
                               sp=(ic == HC - 1 and i3 == 2)):
                            return e.matmul(pb[bank][:, 0:512], lh, rh, start=st, stop=sp)
                        top(wm, reads=[r_acatP, rwo], writes=[pres[bank]])
        for t128 in range(4):
            for dhh in range(2):
                bank = 2 * t128 + dhh
                def accadd(e, bank=bank, t128=t128, dhh=dhh):
                    sl = acc[:, t128 * D + dhh * 512: t128 * D + (dhh + 1) * 512]
                    return e.tensor_tensor(sl, sl, pb[bank][:, 0:512], ALU.add)
                vop(accadd, reads=[pres[bank], r_acc], writes=[r_acc])

        gw3h = gwT_h.rearrange("p (hc e) -> p hc e", hc=HC)
        gw3l = gwT_l.rearrange("p (hc e) -> p hc e", hc=HC)
        xnf3h = xnf_h.rearrange("p (hc t) -> p hc t", hc=HC)
        xnf3l = xnf_l.rearrange("p (hc t) -> p hc t", hc=HC)
        for ti in range(QT):
            bank = 6
            pairs = [(xnf3h[:, hc, ti * 128:(ti + 1) * 128], xnf3l[:, hc, ti * 128:(ti + 1) * 128],
                      gw3h[:, hc, :], gw3l[:, hc, :]) for hc in range(HC)]
            mmgroup3(pb[bank][:, 0:NE], bank, pairs, [r_xnf, r_gwT])
            sop(lambda e, bank=bank: e.activation(g32a, pb[bank][:, 0:NE], AF.Exp, scale=-1.0),
                reads=[pres[bank]], writes=[r_g32a])
            vop(lambda e: e.tensor_scalar(g32a, g32a, 1.0, None, ALU.add),
                reads=[r_g32a], writes=[r_g32a])
            vop(lambda e: e.reciprocal(g32a, g32a), reads=[r_g32a], writes=[r_g32a])
            vop(lambda e: e.tensor_tensor(g32b, g32a, biasb_sb, ALU.add),
                reads=[r_g32a, r_biasb], writes=[r_g32b])
            g32b3 = g32b.rearrange("p (g k) -> p g k", g=4)
            g32c3 = g32c.rearrange("p (g k) -> p g k", g=4)
            for gi in range(4):
                vop(lambda e, gi=gi: e.max(out=g8a, in_=g32b[:, gi * 8:(gi + 1) * 8]),
                    reads=[r_g32b], writes=[r_g8a])
                vop(lambda e, gi=gi: e.tensor_tensor(g4a[:, gi:gi + 1], g8a[:, 0:1],
                                                     g8a[:, 1:2], ALU.add),
                    reads=[r_g8a], writes=[r_g4a])
            vop(lambda e: e.tensor_copy(g8pad[:, 0:4], g4a),
                reads=[r_g4a], writes=[r_g8pad])
            vop(lambda e: e.max(out=g8a, in_=g8pad),
                reads=[r_g8pad], writes=[r_g8a])
            vop(lambda e: e.tensor_scalar(g4b, g4a, g8a[:, 1:2], None, ALU.is_ge),
                reads=[r_g4a, r_g8a], writes=[r_g4b])
            vop(lambda e: e.tensor_tensor(g32c3, g32b3, g4b.unsqueeze(2).broadcast_to([128, 4, 8]),
                                          ALU.mult),
                reads=[r_g32b, r_g4b], writes=[r_g32c])
            vop(lambda e: e.max(out=g8a, in_=g32c), reads=[r_g32c], writes=[r_g8a])
            vop(lambda e: e.tensor_scalar(g32c, g32c, g8a[:, 7:8], None, ALU.is_ge),
                reads=[r_g32c, r_g8a], writes=[r_g32c])
            vop(lambda e: e.tensor_tensor(g32c, g32c, g32a, ALU.mult),
                reads=[r_g32c, r_g32a], writes=[r_g32c])
            vop(lambda e: e.tensor_reduce(g1a, g32c, AX.X, ALU.add),
                reads=[r_g32c], writes=[r_g1a])
            vop(lambda e: e.tensor_scalar(g1a, g1a, 1e-20, None, ALU.add),
                reads=[r_g1a], writes=[r_g1a])
            vop(lambda e: e.reciprocal(g1a, g1a), reads=[r_g1a], writes=[r_g1a])
            def wfo(e, ti=ti):
                return e.tensor_scalar(wf[:, ti * NE:(ti + 1) * NE], g32c, g1a, 2.5,
                                       ALU.mult, ALU.mult)
            vop(wfo, reads=[r_g32c, r_g1a], writes=[r_wf])

        REG.reset()
        shp = []
        for i in range(2):
            ap, r = REG.alloc(f"shp{i}", [128, 3 * D], wdt)
            shp.append((ap, (r, REG.extra(f"su{i}"), REG.extra(f"sd{i}"))))
        pend_sh = None
        for jc in range(HC):
            sp_ap, sp_rs = shp[jc % 2]
            sp_rg, sp_ru, sp_rd = sp_rs
            spb = sp_ap.bitcast(BF16)
            def ldg_(e, spb=spb, jc=jc):
                return e.dma_start(
                    out=spb[:, 0:2 * D].rearrange("p (g f) -> p g f", g=2 * HC),
                    in_=sgT[:, :, jc * 128:(jc + 1) * 128].rearrange("h (a p) f -> p h a f", p=128).opt())
            p.op("sync", ldg_, also=[ph0_last['d_ph0a']], writes=[sp_rg], dma=True)
            def ldu_(e, spb=spb, jc=jc):
                return e.dma_start(
                    out=spb[:, 2 * D:4 * D].rearrange("p (g f) -> p g f", g=2 * HC),
                    in_=suT[:, :, jc * 128:(jc + 1) * 128].rearrange("h (a p) f -> p h a f", p=128).opt())
            p.op("sync", ldu_, also=[ph0_last['d_ph0a']], writes=[sp_ru], dma=True)
            def ldd_(e, spb=spb, jc=jc):
                return e.dma_start(
                    out=spb[:, 4 * D:6 * D].rearrange("p (h f) -> p h f", h=2),
                    in_=sdT[:, jc * 128:(jc + 1) * 128, :].rearrange("h p f -> p h f"))
            p.op("sync", ldd_, also=[ph0_last['d_ph0a']], writes=[sp_rd], dma=True)
            sg3h = spb[:, 0:D].rearrange("p (a f) -> p a f", a=HC)
            sg3l = spb[:, D:2 * D].rearrange("p (a f) -> p a f", a=HC)
            su3h = spb[:, 2 * D:3 * D].rearrange("p (a f) -> p a f", a=HC)
            su3l = spb[:, 3 * D:4 * D].rearrange("p (a f) -> p a f", a=HC)
            pairs_g = [(sg3h[:, hc, :], sg3l[:, hc, :], xnf3h[:, hc, :], xnf3l[:, hc, :])
                       for hc in range(HC)]
            pairs_u = [(su3h[:, hc, :], su3l[:, hc, :], xnf3h[:, hc, :], xnf3l[:, hc, :])
                       for hc in range(HC)]
            mmgroup3(pb[0][:, 0:512], 0, pairs_g, [sp_rg, r_xnf])
            mmgroup3(pb[1][:, 0:512], 1, pairs_u, [sp_ru, r_xnf])
            hsj_h, hsj_l, r_hsj = hsjb[jc % 2]
            sop(lambda e: e.activation(gsil, pb[0][:, 0:512], AF.Silu),
                reads=[pres[0]], writes=[r_gsil])
            vop(lambda e: e.tensor_tensor(hsjF, gsil, pb[1][:, 0:512], ALU.mult),
                reads=[r_gsil, pres[1]], writes=[r_hsjF])
            vsplit(hsjF, r_hsjF, hsj_h, hsj_l, r_hsj, on_scalar_hi=True)

            def mkdowns(hsj_h=hsj_h, hsj_l=hsj_l, r_hsj=r_hsj, spb=spb, sp_r=sp_rd):
                def emit():
                    for t128 in range(4):
                        for dhh in range(2):
                            bank = 2 + (t128 * 2 + dhh) % 2
                            hh = hsj_h[:, t128 * 128:(t128 + 1) * 128]
                            hl = hsj_l[:, t128 * 128:(t128 + 1) * 128]
                            dh_ = spb[:, 4 * D + dhh * 512: 4 * D + (dhh + 1) * 512]
                            dl_ = spb[:, 5 * D + dhh * 512: 5 * D + (dhh + 1) * 512]
                            mmgroup3(pb[bank][:, 0:512], bank,
                                     [(hh, hl, dh_, dl_)], [r_hsj, sp_r])
                            def accadd2(e, bank=bank, t128=t128, dhh=dhh):
                                sl = acc[:, t128 * D + dhh * 512: t128 * D + (dhh + 1) * 512]
                                return e.tensor_tensor(sl, sl, pb[bank][:, 0:512], ALU.add)
                            vop(accadd2, reads=[pres[bank], r_acc], writes=[r_acc])
                return emit
            if pend_sh is not None:
                pend_sh()
            pend_sh = mkdowns()

        if pend_sh is not None:
            pend_sh()

        REG.reset()
        ewb = []
        for i in range(2):
            eg_ap, eg_r = REG.alloc(f"ew{i}_eg", [128, HC * 256], wdt)
            eu_ap, eu_r = REG.alloc(f"ew{i}_eu", [128, HC * 256], wdt)
            ed_ap, ed_r = REG.alloc(f"ew{i}_ed", [128, 2 * D], wdt)
            ewb.append((eg_ap, eu_ap, ed_ap, eg_r, eu_r, ed_r, REG.extra(f"edl{i}")))
        pend_ex = None
        for e_i in range(NE):
            for iic in range(2):
                bi = (2 * e_i + iic) % 2
                eg_ap, eu_ap, ed_ap, eg_r, eu_r, ed_r, ed_r2 = ewb[bi]
                egb = eg_ap.bitcast(BF16)
                eub = eu_ap.bitcast(BF16)
                edb = ed_ap.bitcast(BF16)
                EH = HC * 256
                def ldg(e, egb=egb, e_i=e_i, iic=iic):
                    return e.dma_start(
                        out=egb.rearrange("p (g f) -> p g f", g=2 * HC),
                        in_=egT[e_i][:, :, iic * 256:(iic + 1) * 256].rearrange(
                            "h (a p) f -> p h a f", p=128).opt())
                p.op("sync", ldg, also=[ph0_last['d_ph0b']], writes=[eg_r], dma=True)
                def ldu(e, eub=eub, e_i=e_i, iic=iic):
                    return e.dma_start(
                        out=eub.rearrange("p (g f) -> p g f", g=2 * HC),
                        in_=euT[e_i][:, :, iic * 256:(iic + 1) * 256].rearrange(
                            "h (a p) f -> p h a f", p=128).opt())
                p.op("sync", ldu, also=[ph0_last['d_ph0b']], writes=[eu_r], dma=True)
                def ldd_h(e, edb=edb, e_i=e_i, iic=iic):
                    return e.dma_start(
                        out=edb[:, 0:2 * D].rearrange("p (a f) -> p a f", a=2),
                        in_=edT[e_i][0][iic * 256:(iic + 1) * 256, :].rearrange(
                            "(a p) f -> p a f", p=128))
                p.op("sync", ldd_h, also=[ph0_last['d_ph0b']], writes=[ed_r], dma=True)
                def ldd_l(e, edb=edb, e_i=e_i, iic=iic):
                    return e.dma_start(
                        out=edb[:, 2 * D:4 * D].rearrange("p (a f) -> p a f", a=2),
                        in_=edT[e_i][1][iic * 256:(iic + 1) * 256, :].rearrange(
                            "(a p) f -> p a f", p=128))
                p.op("sync", ldd_l, also=[ph0_last['d_ph0b']], writes=[ed_r2], dma=True)
                he_h, he_l, r_he = heb[(2 * e_i + iic) % 2]
                eg3h = egb[:, 0:EH].rearrange("p (hc i) -> p hc i", hc=HC)
                eg3l = egb[:, EH:2 * EH].rearrange("p (hc i) -> p hc i", hc=HC)
                eu3h = eub[:, 0:EH].rearrange("p (hc i) -> p hc i", hc=HC)
                eu3l = eub[:, EH:2 * EH].rearrange("p (hc i) -> p hc i", hc=HC)
                for i2 in range(2):
                    bg = i2 % 2
                    bu = 2 + (i2 % 2)
                    pairs_g = [(eg3h[:, hc, i2 * 128:(i2 + 1) * 128], eg3l[:, hc, i2 * 128:(i2 + 1) * 128],
                                xnf3h[:, hc, :], xnf3l[:, hc, :]) for hc in range(HC)]
                    pairs_u = [(eu3h[:, hc, i2 * 128:(i2 + 1) * 128], eu3l[:, hc, i2 * 128:(i2 + 1) * 128],
                                xnf3h[:, hc, :], xnf3l[:, hc, :]) for hc in range(HC)]
                    mmgroup3(pb[bg][:, 0:512], bg, pairs_g, [eg_r, r_xnf])
                    mmgroup3(pb[bu][:, 0:512], bu, pairs_u, [eu_r, r_xnf])
                    sop(lambda e, bg=bg: e.activation(gsil, pb[bg][:, 0:512], AF.Silu),
                        reads=[pres[bg]], writes=[r_gsil])
                    def hemul(e, bu=bu):
                        return e.tensor_tensor(heF, gsil, pb[bu][:, 0:512], ALU.mult)
                    vop(hemul, reads=[r_gsil, pres[bu]], writes=[r_heF])
                    vsplit(heF, r_heF, he_h[:, i2 * 512:(i2 + 1) * 512],
                           he_l[:, i2 * 512:(i2 + 1) * 512], r_he, on_scalar_hi=True)

                def mk_edowns(he_h=he_h, he_l=he_l, r_he=r_he, edb=edb, ed_r=ed_r,
                              ed_r2=ed_r2, e_i=e_i):
                    def emit():
                        for t128 in range(4):
                            for dhh in range(2):
                                bank = 4 + (t128 * 2 + dhh) % 4
                                pairs = [(he_h[:, i2 * 512 + t128 * 128: i2 * 512 + t128 * 128 + 128],
                                          he_l[:, i2 * 512 + t128 * 128: i2 * 512 + t128 * 128 + 128],
                                          edb[:, i2 * D + dhh * 512: i2 * D + (dhh + 1) * 512],
                                          edb[:, 2 * D + i2 * D + dhh * 512: 2 * D + i2 * D + (dhh + 1) * 512])
                                         for i2 in range(2)]
                                mmgroup3(pb[bank][:, 0:512], bank, pairs, [r_he, ed_r, ed_r2])
                                def accw(e, bank=bank, t128=t128, dhh=dhh, e_i=e_i):
                                    sl = acc[:, t128 * D + dhh * 512: t128 * D + (dhh + 1) * 512]
                                    return e.scalar_tensor_tensor(
                                        sl, pb[bank][:, 0:512],
                                        wf[:, t128 * NE + e_i: t128 * NE + e_i + 1],
                                        sl, ALU.mult, ALU.add)
                                vop(accw, reads=[pres[bank], r_wf, r_acc], writes=[r_acc])
                    return emit
                if pend_ex is not None:
                    pend_ex()
                pend_ex = mk_edowns()

        if pend_ex is not None:
            pend_ex()

        st = dma(out_d[q * QT:(q + 1) * QT].rearrange("t p f -> p t f"),
                 acc.rearrange("p (t f) -> p t f", t=QT),
                 reads=[r_acc], q="gpsimd", dsem="d_out")
        st_toks.append(st)

    dbg_map = dict(
        kbuf=(kbuf_h, r_kbuf), vbuf=(vbuf_h, r_vbuf), negB=(negB_sb, r_negB),
        nkhT=(nkhT4_h, r_nkhT), gwT=(gwT_h, r_gwT), xnf=(xnf_h, r_xnf),
        qhat=(qhat_h, r_qhat), wf=(wf, r_wf),
    )
    for name in debug:
        ap, rr = dbg_map[name]
        dt = nc.dram_tensor("dbg_" + name, list(ap.shape), ap.dtype, kind="ExternalOutput").ap()
        st_toks.append(dma(dt, ap, reads=[rr], q="gpsimd", dsem="d_out"))
    p.op("sync", None, also=st_toks)
    n, nw = p.finalize_and_emit()
    print(f"program: {n} instructions, {nw} waits")
    return nc


def make_inputs(inputs, core):
    b = core // 2
    par = core % 2
    x = np.asarray(inputs["L0_attn_x"])
    xs = np.ascontiguousarray(x[b].reshape(SEQT, 128, D))
    xo = np.ascontiguousarray(xs[par::2])
    qs = np.asarray(inputs["L0_attn_q_scale"])
    ks = np.asarray(inputs["L0_attn_k_scale"])
    nkv = np.asarray(inputs["L0_attn_null_kv"])

    tri = np.zeros((128, 128), np.float32)
    for t in range(128):
        tri[: t + 1, t] = 1.0
    tri0 = tri.copy()
    tri0[0:64, 0:64] = 1.0
    ones = np.ones((128, 128), np.float32)
    zeros = np.zeros((128, 128), np.float32)
    if par == 0:
        maskA, maskA0, maskB = tri, tri0, zeros
    else:
        maskA, maskA0, maskB = ones, ones, tri

    im = {
        "x_own": xo,
        "x_seq": xs,
        "wq": np.asarray(inputs["L0_attn_Wq"]),
        "wkv": np.asarray(inputs["L0_attn_Wkv"]),
        "wout": np.asarray(inputs["L0_attn_Wout"]),
        "gammab": np.broadcast_to(np.asarray(inputs["L0_attn_gamma"])[None, :], (128, D)),
        "qs_col": np.tile(qs, 2)[:, None],
        "ks_col": ks[:, None],
        "qs_row": qs[None, :],
        "ks_row": ks[None, :],
        "ks2": np.broadcast_to(ks[None, :], (2, DH)),
        "nullk": nkv[0],
        "nullv": nkv[1],
        "gate_w": np.asarray(inputs["L0_ffn_gate_w"]),
        "biasb": np.broadcast_to(np.asarray(inputs["L0_ffn_gate_bias"])[None, :], (128, NE)),
        "eg": np.asarray(inputs["L0_ffn_expert_gate"]),
        "eu": np.asarray(inputs["L0_ffn_expert_up"]),
        "ed": np.asarray(inputs["L0_ffn_expert_down"]),
        "sg": np.asarray(inputs["L0_ffn_shared_gate"]),
        "su": np.asarray(inputs["L0_ffn_shared_up"]),
        "sd": np.asarray(inputs["L0_ffn_shared_down"]),
        "ident": np.eye(128, dtype=np.float32),
        "pool2": (np.arange(128)[:, None] // 64 == np.arange(2)[None, :]).astype(np.float32),
        "hm2": np.concatenate([
            (np.arange(128)[None, :] // 64 == np.arange(2)[:, None]).astype(np.float32),
            np.zeros((30, 128), np.float32),
            (np.arange(128)[None, :] // 64 == np.arange(2)[:, None]).astype(np.float32),
        ], 0),
        "ones64r": np.ones((1, 64), np.float32),
        "ones128r": np.ones((1, 128), np.float32),
        "maskA": maskA, "maskA0": maskA0, "maskB": maskB,
        "dup64": (np.arange(128)[None, :] % 64 == np.arange(64)[:, None]).astype(np.float32),
    }
    return {k: np.ascontiguousarray(v, dtype=np.float32) for k, v in im.items()}


def gather_output(results):
    out = np.empty((4, SEQT, 128, D), np.float32)
    for core in range(8):
        b, par = core // 2, core % 2
        out[b, par::2] = results[core]["out"]
    return out.reshape(4, 4096, D)


from concourse.bass_utils import run_bass_kernel_spmd


def kernel(**inputs):
    nc = bass.Bass("TRN2", target_bir_lowering=False, detect_race_conditions=False)
    build(nc, mode="f32")
    in_maps = [make_inputs(inputs, c) for c in range(8)]
    res = run_bass_kernel_spmd(nc, in_maps, core_ids=list(range(8)))
    return gather_output(res.results)
